# version 06 (final)
# speedup vs baseline: 1.4323x; 1.0591x over previous
import math
import numpy as np
import ml_dtypes
import jax as _jax
try:
    _jax.config.update("jax_default_device", _jax.devices("cpu")[0])
except Exception:
    pass
import concourse.bass as bass
import concourse.mybir as mybir
from concourse.bass_utils import run_bass_kernel_spmd
from contextlib import ExitStack

F32 = mybir.dt.float32
BF16 = mybir.dt.bfloat16
I32 = mybir.dt.int32
AF = mybir.ActivationFunctionType
ALU = mybir.AluOpType
AX = mybir.AxisListType
BF16NP = ml_dtypes.bfloat16

B, T, D = 4, 4096, 1024
NTOK = B * T
CORES = 8
TOK_C = NTOK // CORES
HEADS, DH = 16, 64
INNER = 1024
DI, DIP = 2730, 2816
WIN = 256
HID1 = 3072
NCT = D // 128

RS2 = 1.0 / math.sqrt(2.0)


class Ctr:
    def __init__(self):
        self.c = {}

    def inc(self, n, k):
        self.c[n] = self.c.get(n, 0) + k

    def get(self, n):
        return self.c.get(n, 0)


def cm(x_hbm, p=128):
    return x_hbm.rearrange("(c p) n -> p c n", p=p)


class P:

    def __init__(self, nc, st):
        self.nc = nc
        self.ctr = Ctr()
        self.sem = {}
        for nm in ("Ld", "Lg", "Tm", "Ve", "Se", "Pe", "Ou"):
            self.sem[nm] = st.enter_context(nc.semaphore(nm))
        self.fscr = st.enter_context(nc.sbuf_tensor("fscr", [128, 256], F32))
        self.ftr = st.enter_context(nc.sbuf_tensor("ftr", [128, 8], F32))
        self.block = st.enter_context(nc.Block())

    def vf(self, v):
        m = v.memset(self.fscr[:, :], 0.0)
        m.then_inc(self.sem["Ve"], 1)
        self.ctr.inc("Ve", 1)
        v.wait_ge(self.sem["Ve"], self.ctr.get("Ve"))
        return v.memset(self.ftr[:, 0:4], 0.0)

    def sf(self, s):
        m = s.memzero(self.fscr[:, :])
        m.then_inc(self.sem["Se"], 1)
        self.ctr.inc("Se", 1)
        s.wait_ge(self.sem["Se"], self.ctr.get("Se"))
        return s.memzero(self.ftr[:, 4:8])

    def S(self, f):
        self.block.sync(f)

    def G(self, f):
        self.block.gpsimd(f)

    def V(self, f):
        self.block.vector(f)

    def A(self, f):
        self.block.scalar(f)

    def T(self, f):
        self.block.tensor(f)

    def w(self, eng, *deps):
        full = []
        for d in deps:
            full.append(d)
            if d == "Ld":
                full.append("Lg")
        for d in full:
            if self.ctr.get(d) > 0:
                eng.wait_ge(self.sem[d], self.ctr.get(d))

    def i(self, instr, nm, k=1):
        instr.then_inc(self.sem[nm], k)
        self.ctr.inc(nm, k)


CONST_VALS = [0.5, -0.5, 2.0, -1.0, 1e-5, 1e-6, RS2, 1.0 / 1024.0, 1.0 / 127.0, 1e-38]


def setup_consts(nc, st, p, extra=()):
    vals = CONST_VALS + list(extra)
    vals = sorted(set(float(v) for v in vals))
    cst = st.enter_context(nc.sbuf_tensor("cst", [128, len(vals)], F32))
    for j, v in enumerate(vals):
        try:
            nc.const_aps.register(F32, v, cst[:, j:j + 1])
        except Exception:
            pass

    def emit(p):
        def _(v):
            last = None
            for j, val in enumerate(vals):
                last = v.memset(cst[:, j:j + 1], val)
            p.i(last, "Ve")
        p.V(_)
    return cst, emit


def emit_lnpair(p, ps_s1, ps_s2, stat, CHX, dbg_pre=None):
    def _(v):
        p.w(v, "Tm")
        v.tensor_scalar(stat[0:1, 0:CHX], ps_s1[0:1, 0:CHX], 1.0 / 1024.0, None, ALU.mult)
        v.tensor_scalar(stat[0:1, 2 * CHX:3 * CHX], ps_s2[0:1, 0:CHX], 1.0 / 1024.0, None, ALU.mult)
        if CHX < 128:
            p.vf(v)
        v.tensor_tensor(stat[0:1, 3 * CHX:4 * CHX], stat[0:1, 0:CHX], stat[0:1, 0:CHX], ALU.mult)
        if CHX < 128:
            p.vf(v)
        v.tensor_tensor(stat[0:1, CHX:2 * CHX], stat[0:1, 2 * CHX:3 * CHX],
                        stat[0:1, 3 * CHX:4 * CHX], ALU.subtract)
        p.i(p.vf(v), "Ve")
    p.V(_)

    def _(s):
        p.w(s, "Ve")
        if dbg_pre is not None:
            s.copy(dbg_pre, stat[0:1, CHX:2 * CHX])
        s.activation(stat[0:1, CHX:2 * CHX], stat[0:1, CHX:2 * CHX], AF.Ln, bias=1e-5)
        if CHX < 128:
            p.sf(s)
        s.activation(stat[0:1, CHX:2 * CHX], stat[0:1, CHX:2 * CHX], AF.Exp, scale=-0.5)
        p.i(p.sf(s), "Se")
    p.A(_)


def emit_bcast2(p, ps_bc, onesr, stat, bc_mu, bc_is, CHX):
    def _(t):
        p.w(t, "Se", "Ve")
        p.i(t.matmul(ps_bc[:, 0:CHX], onesr[0:1, 0:128], stat[0:1, 0:CHX],
                     start=True, stop=True), "Tm")
    p.T(_)

    def _(v):
        p.w(v, "Tm")
        last = v.tensor_copy(bc_mu[:, 0:CHX], ps_bc[:, 0:CHX])
        p.i(last if CHX >= 128 else p.vf(v), "Ve")
    p.V(_)

    def _(t):
        p.w(t, "Ve")
        p.i(t.matmul(ps_bc[:, 0:CHX], onesr[0:1, 0:128], stat[0:1, CHX:2 * CHX],
                     start=True, stop=True), "Tm")
    p.T(_)

    def _(v):
        p.w(v, "Tm")
        last = v.tensor_copy(bc_is[:, 0:CHX], ps_bc[:, 0:CHX])
        p.i(last if CHX >= 128 else p.vf(v), "Ve")
    p.V(_)


def emit_stats_mm(p, ps_s1, ps_s2, onec, xb_ct, xsq_ct, W, deps=("Ld", "Se", "Ve")):
    def _(t):
        p.w(t, *deps)
        for ct in range(NCT):
            t.matmul(ps_s1[0:1, 0:W], onec[:, :], xb_ct(ct),
                     start=(ct == 0), stop=(ct == NCT - 1))
        last = None
        for ct in range(NCT):
            last = t.matmul(ps_s2[0:1, 0:W], onec[:, :], xsq_ct(ct),
                            start=(ct == 0), stop=(ct == NCT - 1))
        p.i(last, "Tm")
    p.T(_)



def _stage(fz, pre, nc_io_names, mk_nc_io):
    if fz is None:
        nc = bass.Bass("TRN2", target_bir_lowering=False)
        io = mk_nc_io(nc)
        return nc, None, "", io
    io = {k: fz["io"][pre + k] for k in nc_io_names}
    return fz["nc"], fz["p"], pre, io


def build_launch_f(ntok, fz=None, pre="f_"):
    CH = 256
    nch = ntok // CH
    NDT = HID1 // 128
    SC = 1.0 / 127.0
    def mk(nc):
        return {
            "x3": nc.dram_tensor("x3", [D, ntok], F32, kind="ExternalInput").ap(),
            "wgT": nc.dram_tensor("wgT", [D, HID1], I32, kind="ExternalInput").ap(),
            "wuT": nc.dram_tensor("wuT", [D, HID1], I32, kind="ExternalInput").ap(),
            "wdT": nc.dram_tensor("wdT", [HID1, D], I32, kind="ExternalInput").ap(),
            "out": nc.dram_tensor("out", [D, ntok], F32, kind="ExternalOutput").ap(),
        }
    nc, p0, px, io = _stage(fz, pre, ['x3', 'wgT', 'wuT', 'wdT', 'out'], mk)
    x3, wgT, wuT, wdT, out = io["x3"], io["wgT"], io["wuT"], io["wdT"], io["out"]
    with ExitStack() as st:
        sb = lambda n, s, d: st.enter_context(nc.sbuf_tensor(px + n, s, d))
        pst = lambda n, s: st.enter_context(nc.psum_tensor(px + n, s, F32))
        wg_s = sb("wg_s", [128, NCT * HID1], BF16)
        wu_s = sb("wu_s", [128, NCT * HID1], BF16)
        wd_s = sb("wd_s", [128, NDT * D], BF16)
        onec = sb("onec", [128, 1], BF16)
        onesr = sb("onesr", [1, 128], F32)
        xc = sb("xc", [128, NCT * CH], F32)
        xb = sb("xb", [128, NCT * CH], BF16)
        xsq = sb("xsq", [128, NCT * CH], BF16)
        xnb = sb("xnb", [128, NCT * CH], BF16)
        tmpf = sb("tmpf", [128, CH], F32)
        tsil = sb("tsil", [128, CH], F32)
        hb = sb("hb", [128, NDT * CH], BF16)
        yc = sb("yc", [128, NCT * CH], F32)
        yb = xb
        ysq = xsq
        oc = yc
        stat = sb("stat", [1, 4 * CH], F32)
        bc_mu = sb("bc_mu", [128, CH], F32)
        bc_is = sb("bc_is", [128, CH], F32)
        ps_a = pst("ps_a", [128, 512])
        ps_b = pst("ps_b", [128, 512])
        ps_c = pst("ps_c", [128, 512])
        ps_d = pst("ps_d", [128, 512])
        ps_s1 = pst("ps_s1", [1, 512])
        ps_s2 = pst("ps_s2", [1, 512])
        ps_bc = pst("ps_bc", [128, 512])

        p = p0 or P(nc, st)
        if p0 is None:
            cst, emit_csts = setup_consts(nc, st, p)
            emit_csts(p)

        def _(v):
            v.memset(onec[:, :], 1.0)
            p.i(v.memset(onesr[:, :], 1.0), "Ve")
        p.V(_)

        def _(g):
            p.i(g.dma_start(out=wg_s[:, :].rearrange("p (c n) -> p c n", n=HID1),
                            in_=cm(wgT)), "Lg", 16)
            p.i(g.dma_start(out=wu_s[:, :].rearrange("p (c n) -> p c n", n=HID1),
                            in_=cm(wuT)), "Lg", 16)
            p.i(g.dma_start(out=wd_s[:, :].rearrange("p (c n) -> p c n", n=D),
                            in_=cm(wdT)), "Lg", 16)
        p.G(_)

        for ci in range(nch):
            j0 = ci * CH

            def _(sy, j0=j0):
                p.w(sy, "Ve", "Ou")
                p.i(sy.dma_start(out=xc[:, :].rearrange("p (c n) -> p c n", n=CH),
                                 in_=cm(x3)[:, :, j0:j0 + CH]), "Ld", 16)
            p.S(_)

            def _(g, j0=j0):
                p.w(g, "Ve", "Ou")
                p.i(g.dma_start(out=xb[:, :].rearrange("p (c n) -> p c n", n=CH),
                                in_=cm(x3)[:, :, j0:j0 + CH]), "Lg", 16)
            p.G(_)

            def _(v):
                p.w(v, "Ld")
                p.i(v.tensor_tensor(xsq[:, :], xb[:, :], xb[:, :], ALU.mult), "Ve")
            p.V(_)

            emit_stats_mm(p, ps_s1, ps_s2, onec,
                          lambda ct: xb[:, ct * CH:(ct + 1) * CH],
                          lambda ct: xsq[:, ct * CH:(ct + 1) * CH], CH)
            emit_lnpair(p, ps_s1, ps_s2, stat, CH)
            emit_bcast2(p, ps_bc, onesr, stat, bc_mu, bc_is, CH)

            def _(v):
                last = None
                for ct in range(NCT):
                    sl = slice(ct * CH, (ct + 1) * CH)
                    v.tensor_tensor(tmpf[:, :], xc[:, sl], bc_mu[:, :], ALU.subtract)
                    last = v.scalar_tensor_tensor(xnb[:, sl], tmpf[:, :], SC, bc_is[:, :],
                                                  ALU.mult, ALU.mult)
                p.i(last, "Ve")
            p.V(_)

            for dt in range(NDT):
                pg = ps_a if dt % 2 == 0 else ps_c
                pu = ps_b if dt % 2 == 0 else ps_d

                def _(t, dt=dt, pg=pg, pu=pu):
                    p.w(t, "Ve", "Se")
                    for ct in range(NCT):
                        t.matmul(pg[:, 0:CH], wg_s[:, ct * HID1 + dt * 128: ct * HID1 + dt * 128 + 128],
                                 xnb[:, ct * CH:(ct + 1) * CH],
                                 start=(ct == 0), stop=(ct == NCT - 1))
                    last = None
                    for ct in range(NCT):
                        last = t.matmul(pu[:, 0:CH],
                                        wu_s[:, ct * HID1 + dt * 128: ct * HID1 + dt * 128 + 128],
                                        xnb[:, ct * CH:(ct + 1) * CH],
                                        start=(ct == 0), stop=(ct == NCT - 1))
                    p.i(last, "Tm")
                p.T(_)

                def _(s, pg=pg):
                    p.w(s, "Tm", "Ve")
                    p.i(s.activation(tsil[:, :], pg[:, 0:CH], AF.Silu), "Se")
                p.A(_)

                def _(v, dt=dt, pu=pu):
                    p.w(v, "Se")
                    p.i(v.tensor_tensor(hb[:, dt * CH:(dt + 1) * CH], tsil[:, :], pu[:, 0:CH],
                                        ALU.mult), "Ve")
                p.V(_)

            for dt2 in range(NCT):
                pd = ps_a if dt2 % 2 == 0 else ps_b

                def _(t, dt2=dt2, pd=pd):
                    p.w(t, "Ve", "Se")
                    last = None
                    for ht in range(NDT):
                        last = t.matmul(pd[:, 0:CH],
                                        wd_s[:, ht * D + dt2 * 128: ht * D + dt2 * 128 + 128],
                                        hb[:, ht * CH:(ht + 1) * CH],
                                        start=(ht == 0), stop=(ht == NDT - 1))
                    p.i(last, "Tm")
                p.T(_)

                def _(s, dt2=dt2, pd=pd):
                    p.w(s, "Tm")
                    sl = slice(dt2 * CH, (dt2 + 1) * CH)
                    s.activation(yc[:, sl], pd[:, 0:CH], AF.Copy, scale=SC)
                    p.i(s.activation(yb[:, sl], pd[:, 0:CH], AF.Copy, scale=SC), "Se")
                p.A(_)

            def _(v):
                p.w(v, "Se")
                p.i(v.tensor_tensor(ysq[:, :], yb[:, :], yb[:, :], ALU.mult), "Ve")
            p.V(_)

            emit_stats_mm(p, ps_s1, ps_s2, onec,
                          lambda ct: yb[:, ct * CH:(ct + 1) * CH],
                          lambda ct: ysq[:, ct * CH:(ct + 1) * CH], CH, deps=("Se",))
            emit_lnpair(p, ps_s1, ps_s2, stat, CH)
            emit_bcast2(p, ps_bc, onesr, stat, bc_mu, bc_is, CH)

            def _(v, j0=j0):
                last = None
                for ct in range(NCT):
                    sl = slice(ct * CH, (ct + 1) * CH)
                    v.tensor_tensor(tmpf[:, :], yc[:, sl], bc_mu[:, :], ALU.subtract)
                    v.tensor_tensor(tmpf[:, :], tmpf[:, :], bc_is[:, :], ALU.mult)
                    last = v.tensor_tensor(oc[:, sl], tmpf[:, :], xc[:, sl], ALU.add)
                p.i(last, "Ve")
            p.V(_)

            def _(sy, j0=j0):
                p.w(sy, "Ve")
                p.i(sy.dma_start(out=cm(out)[:, :, j0:j0 + CH],
                                 in_=oc[:, :].rearrange("p (c n) -> p c n", n=CH)), "Ou", 16)
            p.S(_)

        def _(sy):
            p.w(sy, "Ou")
        p.S(_)

    return nc




def build_launch_a(nh=8, fz=None, pre="a_"):
    CH = 512
    nch = T // CH
    NTT = T // 128
    HD = nh * 64
    def mk(nc):
        return {
            "x": nc.dram_tensor("x", [D, T], F32, kind="ExternalInput").ap(),
            "wqT": nc.dram_tensor("wqT", [D, HD], F32, kind="ExternalInput").ap(),
            "wkT": nc.dram_tensor("wkT", [D, HD], F32, kind="ExternalInput").ap(),
            "wvT": nc.dram_tensor("wvT", [D, HD], F32, kind="ExternalInput").ap(),
            "att": nc.dram_tensor("att", [HD, T], F32, kind="ExternalOutput").ap(),
        }
    nc, p0, px, io = _stage(fz, pre, ['x', 'wqT', 'wkT', 'wvT', 'att'], mk)
    x, wqT, wkT, wvT, att = io["x"], io["wqT"], io["wkT"], io["wvT"], io["att"]
    with ExitStack() as st:
        sb = lambda n, s, d: st.enter_context(nc.sbuf_tensor(px + n, s, d))
        pst = lambda n, s: st.enter_context(nc.psum_tensor(px + n, s, F32))
        onec = sb("onec", [128, 1], BF16)
        onesr = sb("onesr", [1, 128], F32)
        wq_s = sb("wq_s", [128, NCT * HD], BF16)
        wk_s = sb("wk_s", [128, NCT * HD], BF16)
        wv_s = sb("wv_s", [128, NCT * HD], BF16)
        xnb = sb("xnb", [128, NCT * T], BF16)
        xc = sb("xc", [128, NCT * CH], F32)
        xb = sb("xb", [128, NCT * CH], BF16)
        xsq = sb("xsq", [128, NCT * CH], BF16)
        tmpf = sb("tmpf", [128, CH], F32)
        tmp2 = sb("tmp2", [128, CH], F32)
        stat = sb("stat", [1, 4 * CH], F32)
        bc_mu = sb("bc_mu", [128, CH], F32)
        bc_is = sb("bc_is", [128, CH], F32)
        kb = sb("kb", [128, NTT * 128], BF16)
        vb = sb("vb", [128, NTT * 128], BF16)
        qb = sb("qb", [128, T], BF16)
        kvb = sb("kvb", [128, 65], BF16)
        atf = sb("atf", [128, CH], F32)
        bcr = sb("bcr", [128, CH], F32)
        dvr = sb("dvr", [1, 2 * CH], F32)
        ps_s1 = pst("ps_s1", [1, 512])
        ps_s2 = pst("ps_s2", [1, 512])
        ps_bc = pst("ps_bc", [128, 512])
        ps_m1 = pst("ps_m1", [128, 512])
        ps_m2 = pst("ps_m2", [128, 512])
        ps_kv = pst("ps_kv", [128, 130])
        ps_a1 = pst("ps_a1", [128, 512])
        ps_a2 = pst("ps_a2", [128, 512])
        p = p0 or P(nc, st)
        if p0 is None:
            cst, emit_csts = setup_consts(nc, st, p)
            emit_csts(p)

        def _(v):
            v.memset(onec[:, :], 1.0)
            p.i(v.memset(onesr[:, :], 1.0), "Ve")
        p.V(_)

        def _(g):
            p.i(g.dma_start(out=wq_s[:, :].rearrange("p (c n) -> p c n", n=HD),
                            in_=cm(wqT)), "Lg", 16)
            p.i(g.dma_start(out=wk_s[:, :].rearrange("p (c n) -> p c n", n=HD),
                            in_=cm(wkT)), "Lg", 16)
            p.i(g.dma_start(out=wv_s[:, :].rearrange("p (c n) -> p c n", n=HD),
                            in_=cm(wvT)), "Lg", 16)
        p.G(_)

        for ci in range(nch):
            j0 = ci * CH

            def _(sy, j0=j0):
                p.w(sy, "Ve")
                p.i(sy.dma_start(out=xc[:, :].rearrange("p (c n) -> p c n", n=CH),
                                 in_=cm(x)[:, :, j0:j0 + CH]), "Ld", 16)
            p.S(_)

            def _(g, j0=j0):
                p.w(g, "Ve")
                p.i(g.dma_start(out=xb[:, :].rearrange("p (c n) -> p c n", n=CH),
                                in_=cm(x)[:, :, j0:j0 + CH]), "Lg", 16)
            p.G(_)

            def _(v):
                p.w(v, "Ld")
                p.i(v.tensor_tensor(xsq[:, :], xb[:, :], xb[:, :], ALU.mult), "Ve")
            p.V(_)

            emit_stats_mm(p, ps_s1, ps_s2, onec,
                          lambda ct: xb[:, ct * CH:(ct + 1) * CH],
                          lambda ct: xsq[:, ct * CH:(ct + 1) * CH], CH)
            emit_lnpair(p, ps_s1, ps_s2, stat, CH)
            emit_bcast2(p, ps_bc, onesr, stat, bc_mu, bc_is, CH)

            def _(v, j0=j0):
                last = None
                for ct in range(NCT):
                    v.tensor_tensor(tmpf[:, :], xc[:, ct * CH:(ct + 1) * CH], bc_mu[:, :],
                                    ALU.subtract)
                    last = v.tensor_tensor(xnb[:, ct * T + j0: ct * T + j0 + CH],
                                           tmpf[:, :], bc_is[:, :], ALU.mult)
                p.i(last, "Ve")
            p.V(_)

        for hp in range(nh // 2):
            d0 = hp * 128

            for tt in range(NTT):
                pk = ps_m1 if tt % 2 == 0 else ps_m2

                def _(t, tt=tt, pk=pk, d0=d0):
                    p.w(t, "Ve", "Se")
                    for ct in range(NCT):
                        t.matmul(pk[:, 0:128],
                                 xnb[:, ct * T + tt * 128: ct * T + tt * 128 + 128],
                                 wk_s[:, ct * HD + d0: ct * HD + d0 + 128],
                                 start=(ct == 0), stop=(ct == NCT - 1))
                    last = None
                    for ct in range(NCT):
                        last = t.matmul(pk[:, 128:256],
                                        xnb[:, ct * T + tt * 128: ct * T + tt * 128 + 128],
                                        wv_s[:, ct * HD + d0: ct * HD + d0 + 128],
                                        start=(ct == 0), stop=(ct == NCT - 1))
                    p.i(last, "Tm")
                p.T(_)

                def _(v, tt=tt, pk=pk):
                    p.w(v, "Tm")
                    v.tensor_scalar(tmpf[:, 0:128], pk[:, 0:128], 0.0, None, ALU.min)
                    p.i(v.tensor_scalar(tmp2[:, 0:128], pk[:, 0:128], 0.0, None, ALU.max),
                        "Ve")
                p.V(_)

                def _(s, tt=tt, pk=pk):
                    p.w(s, "Ve")
                    s.activation(tmpf[:, 0:128], tmpf[:, 0:128], AF.Exp)
                    p.i(s.activation(vb[:, tt * 128: tt * 128 + 128], pk[:, 128:256],
                                     AF.Copy), "Se")
                p.A(_)

                def _(v, tt=tt):
                    p.w(v, "Se")
                    p.i(v.tensor_tensor(kb[:, tt * 128: tt * 128 + 128], tmpf[:, 0:128],
                                        tmp2[:, 0:128], ALU.add), "Ve")
                p.V(_)

            def _(t):
                p.w(t, "Ve", "Se")
                last = None
                for hl in range(2):
                    for tt in range(NTT):
                        t.matmul(ps_kv[0:64, hl * 65: hl * 65 + 64],
                                 kb[:, tt * 128 + hl * 64: tt * 128 + hl * 64 + 64],
                                 vb[:, tt * 128 + hl * 64: tt * 128 + hl * 64 + 64],
                                 start=(tt == 0), stop=(tt == NTT - 1))
                    for tt in range(NTT):
                        last = t.matmul(ps_kv[0:64, hl * 65 + 64: hl * 65 + 65],
                                        kb[:, tt * 128 + hl * 64: tt * 128 + hl * 64 + 64],
                                        onec[:, :],
                                        start=(tt == 0), stop=(tt == NTT - 1))
                p.i(last, "Tm")
            p.T(_)

            def _(s):
                p.w(s, "Tm")
                s.activation(kvb[0:64, 0:65], ps_kv[0:64, 0:65], AF.Copy)
                s.activation(kvb[64:128, 0:65], ps_kv[0:64, 65:130], AF.Copy)
                p.i(p.sf(s), "Se")
            p.A(_)

            for ci in range(nch):
                j0 = ci * CH

                def _(t, j0=j0, d0=d0):
                    p.w(t, "Ve", "Se")
                    last = None
                    for ct in range(NCT):
                        last = t.matmul(ps_m1[:, 0:CH],
                                        wq_s[:, ct * HD + d0: ct * HD + d0 + 128],
                                        xnb[:, ct * T + j0: ct * T + j0 + CH],
                                        start=(ct == 0), stop=(ct == NCT - 1))
                    p.i(last, "Tm")
                p.T(_)

                def _(v):
                    p.w(v, "Tm")
                    v.tensor_scalar(tmpf[:, :], ps_m1[:, 0:CH], 0.0, None, ALU.min)
                    p.i(v.tensor_scalar(tmp2[:, :], ps_m1[:, 0:CH], 0.0, None, ALU.max),
                        "Ve")
                p.V(_)

                def _(s):
                    p.w(s, "Ve")
                    p.i(s.activation(tmpf[:, :], tmpf[:, :], AF.Exp), "Se")
                p.A(_)

                def _(v, j0=j0):
                    p.w(v, "Se")
                    p.i(v.tensor_tensor(qb[:, j0:j0 + CH], tmpf[:, :], tmp2[:, :], ALU.add),
                        "Ve")
                p.V(_)

                def _(t, j0=j0):
                    p.w(t, "Ve")
                    t.matmul(ps_a1[0:65, 0:CH], kvb[0:64, 0:65], qb[0:64, j0:j0 + CH],
                             start=True, stop=True)
                    p.i(t.matmul(ps_a2[0:65, 0:CH], kvb[64:128, 0:65], qb[64:128, j0:j0 + CH],
                                 start=True, stop=True), "Tm")
                p.T(_)

                for hl, pa in ((0, ps_a1), (1, ps_a2)):
                    def _(v, pa=pa, hl=hl):
                        p.w(v, "Tm", "Ou")
                        p.i(v.tensor_scalar(dvr[0:1, hl * CH:(hl + 1) * CH],
                                            pa[64:65, 0:CH], 1e-6, None, ALU.max), "Ve")
                    p.V(_)

                    def _(s, hl=hl):
                        p.w(s, "Ve")
                        s.activation(dvr[0:1, hl * CH:(hl + 1) * CH],
                                     dvr[0:1, hl * CH:(hl + 1) * CH], AF.Ln)
                        p.i(s.activation(dvr[0:1, hl * CH:(hl + 1) * CH],
                                         dvr[0:1, hl * CH:(hl + 1) * CH], AF.Exp,
                                         scale=-1.0), "Se")
                    p.A(_)

                    def _(t, hl=hl):
                        p.w(t, "Se")
                        p.i(t.matmul(ps_bc[0:64, 0:CH], onesr[0:1, 0:64],
                                     dvr[0:1, hl * CH:(hl + 1) * CH], start=True, stop=True),
                            "Tm")
                    p.T(_)

                    def _(v, pa=pa, hl=hl):
                        p.w(v, "Tm")
                        v.tensor_copy(bcr[0:64, 0:CH], ps_bc[0:64, 0:CH])
                        p.i(v.tensor_tensor(atf[hl * 64:(hl + 1) * 64, 0:CH], pa[0:64, 0:CH],
                                            bcr[0:64, 0:CH], ALU.mult), "Ve")
                    p.V(_)

                def _(sy, j0=j0, hp=hp):
                    p.w(sy, "Ve")
                    p.i(sy.dma_start(out=att[hp * 128:(hp + 1) * 128, j0:j0 + CH],
                                     in_=atf[:, 0:CH]), "Ou", 16)
                p.S(_)

        def _(sy):
            p.w(sy, "Ou")
        p.S(_)
    return nc


def build_launch_d(nh=8, fz=None, pre="d_"):
    CH = 512
    nch = T // CH
    NTT = T // 128
    HD = nh * 64
    def mk(nc):
        return {
            "x": nc.dram_tensor("x", [D, T], F32, kind="ExternalInput").ap(),
            "wqT": nc.dram_tensor("wqT", [D, HD], F32, kind="ExternalInput").ap(),
            "wkT": nc.dram_tensor("wkT", [D, HD], F32, kind="ExternalInput").ap(),
            "wvT": nc.dram_tensor("wvT", [D, HD], F32, kind="ExternalInput").ap(),
            "slp": nc.dram_tensor("slp", [1, nh], F32, kind="ExternalInput").ap(),
            "pwv": nc.dram_tensor("pwv", [128, 1], F32, kind="ExternalInput").ap(),
            "bmv": nc.dram_tensor("bmv", [128, 1], F32, kind="ExternalInput").ap(),
            "pos": nc.dram_tensor("pos", [1, T], F32, kind="ExternalInput").ap(),
            "msk": nc.dram_tensor("msk", [128, 4 * 512], F32, kind="ExternalInput").ap(),
            "ctri": nc.dram_tensor("ctri", [128, 4 * 512], F32, kind="ExternalInput").ap(),
            "att": nc.dram_tensor("att", [HD, T], F32, kind="ExternalOutput").ap(),
        }
    nc, p0, px, io = _stage(fz, pre, ['x', 'wqT', 'wkT', 'wvT', 'slp', 'pwv', 'bmv', 'pos', 'msk', 'ctri', 'att'], mk)
    x, wqT, wkT, wvT, slp, pwv, bmv, pos, msk, ctri, att = io["x"], io["wqT"], io["wkT"], io["wvT"], io["slp"], io["pwv"], io["bmv"], io["pos"], io["msk"], io["ctri"], io["att"]
    with ExitStack() as st:
        sb = lambda n, s, d: st.enter_context(nc.sbuf_tensor(px + n, s, d))
        pst = lambda n, s: st.enter_context(nc.psum_tensor(px + n, s, F32))
        onec = sb("onec", [128, 1], BF16)
        onesr = sb("onesr", [1, 512], F32)
        negr = sb("negr", [1, 128], F32)
        wq_s = sb("wq_s", [128, NCT * HD], BF16)
        wk_s = sb("wk_s", [128, NCT * HD], BF16)
        wv_s = sb("wv_s", [128, NCT * HD], BF16)
        xnb = sb("xnb", [128, NCT * T], BF16)
        CH1 = 256
        stat = sb("stat", [1, 4 * CH], F32)
        qb = sb("qb", [128, T], BF16)
        kb = sb("kb", [128, T], BF16)
        vb = sb("vb", [128, NTT * 128], BF16)
        slp_s = sb("slp_s", [1, nh], F32)
        pw_s = sb("pw_s", [128, 1], F32)
        bm_s = sb("bm_s", [128, 1], F32)
        pmk = sb("pmk", [128, 1], F32)
        srow = sb("srow", [1, T], F32)
        srowc = sb("srowc", [128, NTT], F32)
        slpb = sb("slpb", [128, 1], F32)
        msk_s = sb("msk_s", [128, 4 * CH], BF16)
        ctb = sb("ctb", [128, 4 * CH], BF16)
        tot_s = sb("tot_s", [1, 80], F32)
        atf = sb("atf", [128, CH], F32)
        bcr = sb("bcr", [128, CH], F32)
        dvr = sb("dvr", [1, CH], F32)
        ps_bc = pst("ps_bc", [128, 512])
        ps_m1 = pst("ps_m1", [128, 512])
        ps_sc = pst("ps_sc", [128, 512])
        ps_pv = pst("ps_pv", [128, 512])
        ps_tt = pst("ps_tt", [1, 512])
        p = p0 or P(nc, st)
        if p0 is None:
            cst, emit_csts = setup_consts(nc, st, p)
            emit_csts(p)
        st1 = ExitStack()
        ps_s1 = st1.enter_context(nc.psum_tensor(px + "ps_s1", [1, 512], F32))
        ps_s2 = st1.enter_context(nc.psum_tensor(px + "ps_s2", [1, 512], F32))
        st2 = ExitStack()
        sb2 = lambda n, s, d: st2.enter_context(nc.sbuf_tensor(px + n, s, d))
        xc = sb2("xc", [128, NCT * CH1], F32)
        xb = sb2("xb", [128, NCT * CH1], BF16)
        xsq = sb2("xsq", [128, NCT * CH1], BF16)
        tmpf = sb2("tmpf", [128, CH], F32)
        bc_mu = sb2("bc_mu", [128, CH], F32)
        bc_is = sb2("bc_is", [128, CH], F32)

        def _(v):
            v.memset(onec[:, :], 1.0)
            v.memset(onesr[:, :], 1.0)
            p.i(v.memset(negr[:, :], -1.0), "Ve")
        p.V(_)

        def _(g):
            p.i(g.dma_start(out=wq_s[:, :].rearrange("p (c n) -> p c n", n=HD),
                            in_=cm(wqT)), "Lg", 16)
            p.i(g.dma_start(out=wk_s[:, :].rearrange("p (c n) -> p c n", n=HD),
                            in_=cm(wkT)), "Lg", 16)
            p.i(g.dma_start(out=wv_s[:, :].rearrange("p (c n) -> p c n", n=HD),
                            in_=cm(wvT)), "Lg", 16)
        p.G(_)

        def _(sy):
            p.i(sy.dma_start(out=slp_s[:, :], in_=slp), "Ld", 16)
            p.i(sy.dma_start(out=pw_s[:, :], in_=pwv), "Ld", 16)
            p.i(sy.dma_start(out=bm_s[:, :], in_=bmv), "Ld", 16)
        p.S(_)

        def _(g):
            p.i(g.dma_start(out=msk_s[:, :], in_=msk), "Lg", 16)
        p.G(_)

        def _(g):
            p.i(g.dma_start(out=ctb[:, :], in_=ctri), "Lg", 16)
        p.G(_)

        def _(s):
            p.w(s, "Ld", "Ve")
            s.activation(pmk[:, :], bm_s[:, :], AF.Ln)
            p.sf(s)
            s.activation(pmk[:, :], pmk[:, :], AF.Exp, scale=pw_s[:, 0:1])
            p.i(p.sf(s), "Se")
        p.A(_)

        def _(v):
            p.w(v, "Se")
            p.i(v.tensor_scalar(ctb[:, :], ctb[:, :], pmk[:, 0:1], -1.0,
                                ALU.mult, ALU.mult), "Ve")
        p.V(_)

        for ci in range(T // CH1):
            j0 = ci * CH1

            def _(sy, j0=j0):
                p.w(sy, "Ve", "Ou")
                p.i(sy.dma_start(out=xc[:, :].rearrange("p (c n) -> p c n", n=CH1),
                                 in_=cm(x)[:, :, j0:j0 + CH1]), "Ld", 16)
            p.S(_)

            def _(g, j0=j0):
                p.w(g, "Ve", "Ou")
                p.i(g.dma_start(out=xb[:, :].rearrange("p (c n) -> p c n", n=CH1),
                                in_=cm(x)[:, :, j0:j0 + CH1]), "Lg", 16)
            p.G(_)

            def _(v):
                p.w(v, "Ld")
                p.i(v.tensor_tensor(xsq[:, :], xb[:, :], xb[:, :], ALU.mult), "Ve")
            p.V(_)

            emit_stats_mm(p, ps_s1, ps_s2, onec,
                          lambda ct: xb[:, ct * CH1:(ct + 1) * CH1],
                          lambda ct: xsq[:, ct * CH1:(ct + 1) * CH1], CH1)
            emit_lnpair(p, ps_s1, ps_s2, stat, CH1)
            emit_bcast2(p, ps_bc, onesr, stat, bc_mu, bc_is, CH1)

            def _(v, j0=j0):
                last = None
                for ct in range(NCT):
                    v.tensor_tensor(tmpf[:, 0:CH1], xc[:, ct * CH1:(ct + 1) * CH1],
                                    bc_mu[:, 0:CH1], ALU.subtract)
                    last = v.tensor_tensor(xnb[:, ct * T + j0: ct * T + j0 + CH1],
                                           tmpf[:, 0:CH1], bc_is[:, 0:CH1], ALU.mult)
                p.i(last, "Ve")
            p.V(_)

        st2.close()
        st1.close()
        ps_c1 = pst("ps_c1", [128, 512])
        ps_c2 = pst("ps_c2", [128, 512])
        ps_c3 = pst("ps_c3", [128, 512])
        psc = [ps_sc, ps_c1, ps_c2, ps_c3]
        u_t = sb("u_t", [128, 4 * CH], F32)
        a_t = sb("a_t", [128, 4 * CH], F32)
        p_t = sb("p_t", [128, 4 * CH], BF16)

        for hp in range(nh // 2):
            d0 = hp * 128

            for ci in range(nch):
                j0 = ci * CH
                for (wsl, dst, scl) in ((wq_s, qb, 2.0), (wk_s, kb, None)):
                    def _(t, j0=j0, wsl=wsl, d0=d0):
                        p.w(t, "Ve", "Se")
                        last = None
                        for ct in range(NCT):
                            last = t.matmul(ps_m1[:, 0:CH],
                                            wsl[:, ct * HD + d0: ct * HD + d0 + 128],
                                            xnb[:, ct * T + j0: ct * T + j0 + CH],
                                            start=(ct == 0), stop=(ct == NCT - 1))
                        p.i(last, "Tm")
                    p.T(_)

                    def _(v, j0=j0, dst=dst, scl=scl):
                        p.w(v, "Tm")
                        if scl is None:
                            p.i(v.tensor_copy(dst[:, j0:j0 + CH], ps_m1[:, 0:CH]), "Ve")
                        else:
                            p.i(v.tensor_scalar(dst[:, j0:j0 + CH], ps_m1[:, 0:CH], scl,
                                                None, ALU.mult), "Ve")
                    p.V(_)

            for tt in range(NTT):
                def _(t, tt=tt, d0=d0):
                    p.w(t, "Ve", "Se")
                    last = None
                    for ct in range(NCT):
                        last = t.matmul(ps_m1[:, 0:128],
                                        xnb[:, ct * T + tt * 128: ct * T + tt * 128 + 128],
                                        wv_s[:, ct * HD + d0: ct * HD + d0 + 128],
                                        start=(ct == 0), stop=(ct == NCT - 1))
                    p.i(last, "Tm")
                p.T(_)

                def _(v, tt=tt):
                    p.w(v, "Tm")
                    p.i(v.tensor_copy(vb[:, tt * 128: tt * 128 + 128], ps_m1[:, 0:128]),
                        "Ve")
                p.V(_)

            for hl in range(2):
                h = 2 * hp + hl
                vsl = slice(hl * 64, hl * 64 + 64)
                psl = slice(hl * 64, hl * 64 + 64)

                def _(g, h=h):
                    p.w(g, "Tm", "Se", "Ve")
                    g.iota(srow[0:1, :], [[1, T]], channel_multiplier=0,
                           allow_small_or_imprecise_dtypes=True)
                    p.i(g.iota(srowc[:, :], [[128, NTT]], channel_multiplier=1,
                               allow_small_or_imprecise_dtypes=True), "Pe")
                p.G(_)

                def _(t, h=h):
                    p.w(t, "Ld", "Ve")
                    p.i(t.matmul(ps_bc[:, 0:1], onesr[0:1, 0:128], slp_s[0:1, h:h + 1],
                                 start=True, stop=True), "Tm")
                p.T(_)

                def _(v, h=h):
                    p.w(v, "Ld", "Pe", "Tm")
                    v.tensor_copy(slpb[:, :], ps_bc[:, 0:1])
                    p.vf(v)
                    v.tensor_scalar(srow[0:1, :], srow[0:1, :],
                                    slp_s[0:1, h:h + 1], None, ALU.mult)
                    p.i(v.tensor_scalar(srowc[:, :], srowc[:, :], slpb[:, 0:1], None,
                                        ALU.mult), "Ve")
                p.V(_)

                def _(t, hl=hl):
                    p.w(t, "Ve", "Se")
                    for tt in range(NTT):
                        t.matmul(ps_tt[0:1, 0:64], onec[:, :],
                                 vb[:, tt * 128 + hl * 64: tt * 128 + hl * 64 + 64],
                                 start=(tt == 0), stop=(tt == NTT - 1))
                    last = None
                    for tt in range(NTT):
                        last = t.matmul(ps_tt[0:1, 64:65], onec[:, :], onec[:, :],
                                        start=(tt == 0), stop=(tt == NTT - 1))
                    p.i(last, "Tm")
                p.T(_)

                def _(v):
                    p.w(v, "Tm")
                    v.tensor_copy(tot_s[0:1, 0:65], ps_tt[0:1, 0:65])
                    p.i(p.vf(v), "Ve")
                p.V(_)

                for ic in range(nch):
                    i0 = ic * CH
                    njt = i0 // 128 + 4

                    if ic > 0:
                        def _(v):
                            p.w(v, "Tm")
                            v.tensor_copy(stat[0:1, 0:65], ps_tt[0:1, 128:193])
                            p.i(p.vf(v), "Ve")
                        p.V(_)

                    def _(v, ic=ic):
                        if ic == 0:
                            v.tensor_scalar(stat[0:1, 128:193], tot_s[0:1, 0:65],
                                            pmk[0:1, 0:1], None, ALU.mult)
                        else:
                            v.tensor_tensor(stat[0:1, 256:321], tot_s[0:1, 0:65],
                                            stat[0:1, 0:65], ALU.subtract)
                            p.vf(v)
                            v.tensor_scalar(stat[0:1, 128:193], stat[0:1, 256:321],
                                            pmk[0:1, 0:1], None, ALU.mult)
                        p.i(p.vf(v), "Ve")
                    p.V(_)

                    first = {"num": True, "den": True}

                    def mmacc(t, out_ap, lhsT, rhs, region, stop=False):
                        t.matmul(out_ap, lhsT, rhs, start=first[region], stop=stop)
                        first[region] = False

                    assert njt % 4 == 0
                    for bq in range(njt // 4):
                        jts = [4 * bq + q for q in range(4)]

                        def _(t, jts=jts, i0=i0, psl=psl):
                            p.w(t, "Ve", "Se")
                            last = None
                            for q, jt in enumerate(jts):
                                j0t = jt * 128
                                t.matmul(psc[q][:, 0:CH], kb[psl, j0t:j0t + 128],
                                         qb[psl, i0:i0 + CH], start=True, stop=False)
                                last = t.matmul(psc[q][:, 0:CH], negr[0:1, 0:128],
                                                srow[0:1, i0:i0 + CH], start=False,
                                                stop=True)
                            p.i(last, "Tm")
                        p.T(_)

                        if jts[-1] * 128 - i0 >= 0:
                            def _(v, jts=jts, i0=i0):
                                p.w(v, "Tm")
                                last = None
                                for q, jt in enumerate(jts):
                                    e0 = jt * 128 - i0
                                    if e0 >= 0:
                                        last = v.tensor_tensor(
                                            psc[q][:, 0:CH], psc[q][:, 0:CH],
                                            msk_s[:, (e0 // 128) * CH:
                                                  (e0 // 128) * CH + CH], ALU.add)
                                p.i(last, "Ve")
                            p.V(_)

                        def _(s, jts=jts):
                            p.w(s, "Tm", "Ve")
                            last = None
                            for q, jt in enumerate(jts):
                                last = s.activation(u_t[:, q * CH:(q + 1) * CH],
                                                    psc[q][:, 0:CH], AF.Abs,
                                                    bias=srowc[:, jt:jt + 1])
                            p.i(last, "Se")
                        p.A(_)

                        def _(v, jts=jts):
                            p.w(v, "Se")
                            last = None
                            for q, jt in enumerate(jts):
                                last = v.scalar_tensor_tensor(
                                    a_t[:, q * CH:(q + 1) * CH], psc[q][:, 0:CH],
                                    srowc[:, jt:jt + 1],
                                    u_t[:, q * CH:(q + 1) * CH], ALU.add, ALU.add)
                            p.i(last, "Ve")
                        p.V(_)

                        def _(s):
                            p.w(s, "Ve")
                            last = None
                            for q in range(4):
                                s.activation(a_t[:, q * CH:(q + 1) * CH],
                                             a_t[:, q * CH:(q + 1) * CH], AF.Ln,
                                             bias=1.0)
                                last = s.activation(u_t[:, q * CH:(q + 1) * CH],
                                                    u_t[:, q * CH:(q + 1) * CH],
                                                    AF.Ln, scale=2.0, bias=2.0)
                            p.i(last, "Se")
                        p.A(_)

                        def _(v):
                            p.w(v, "Se")
                            last = None
                            for q in range(4):
                                last = v.tensor_tensor(a_t[:, q * CH:(q + 1) * CH],
                                                       a_t[:, q * CH:(q + 1) * CH],
                                                       u_t[:, q * CH:(q + 1) * CH],
                                                       ALU.subtract)
                            p.i(last, "Ve")
                        p.V(_)

                        def _(s):
                            p.w(s, "Ve")
                            last = None
                            for q in range(4):
                                last = s.activation(p_t[:, q * CH:(q + 1) * CH],
                                                    a_t[:, q * CH:(q + 1) * CH],
                                                    AF.Exp, scale=pw_s[:, 0:1])
                            p.i(last, "Se")
                        p.A(_)

                        def _(t, jts=jts, hl=hl, i0=i0):
                            p.w(t, "Se", "Ve")
                            for q, jt in enumerate(jts):
                                e0 = jt * 128 - i0
                                pt = p_t[:, q * CH:(q + 1) * CH]
                                mmacc(t, ps_pv[0:64, 0:CH],
                                      vb[:, jt * 128 + hl * 64: jt * 128 + hl * 64 + 64],
                                      pt, "num")
                                mmacc(t, ps_pv[64:65, 0:CH], onec[:, :], pt, "den")
                                if e0 >= 0:
                                    ii = e0 // 128
                                    mmacc(t, ps_pv[0:64, 0:CH],
                                          vb[:, jt * 128 + hl * 64: jt * 128 + hl * 64 + 64],
                                          ctb[:, ii * CH:(ii + 1) * CH], "num")
                                    mmacc(t, ps_pv[64:65, 0:CH], onec[:, :],
                                          ctb[:, ii * CH:(ii + 1) * CH], "den")
                            p.i(t.nop(), "Tm")
                        p.T(_)

                    def _(t, ic=ic, hl=hl):
                        p.w(t, "Ve", "Se")
                        mmacc(t, ps_pv[0:65, 0:CH], stat[0:1, 128:193], onesr[0:1, 0:CH],
                              "num", stop=True)
                        last = None
                        for jt in range(ic * 4, ic * 4 + 4):
                            fin = (ic == nch - 1 and jt == ic * 4 + 3)
                            t.matmul(ps_tt[0:1, 128:192], onec[:, :],
                                     vb[:, jt * 128 + hl * 64: jt * 128 + hl * 64 + 64],
                                     start=(ic == 0 and jt == ic * 4), stop=fin)
                            last = t.matmul(ps_tt[0:1, 192:193], onec[:, :], onec[:, :],
                                            start=(ic == 0 and jt == ic * 4), stop=fin)
                        p.i(last, "Tm")
                    p.T(_)

                    def _(v):
                        p.w(v, "Tm", "Ou")
                        p.i(v.tensor_scalar(dvr[0:1, :], ps_pv[64:65, 0:CH], 1e-6, None,
                                            ALU.add), "Ve")
                    p.V(_)

                    def _(s):
                        p.w(s, "Ve")
                        s.activation(dvr[0:1, :], dvr[0:1, :], AF.Ln)
                        p.i(s.activation(dvr[0:1, :], dvr[0:1, :], AF.Exp, scale=-1.0),
                            "Se")
                    p.A(_)

                    def _(t):
                        p.w(t, "Se")
                        p.i(t.matmul(ps_bc[0:64, 0:CH], onesr[0:1, 0:64], dvr[0:1, :],
                                     start=True, stop=True), "Tm")
                    p.T(_)

                    def _(v):
                        p.w(v, "Tm")
                        v.tensor_copy(bcr[0:64, 0:CH], ps_bc[0:64, 0:CH])
                        p.i(v.tensor_tensor(atf[0:64, 0:CH], ps_pv[0:64, 0:CH],
                                            bcr[0:64, 0:CH], ALU.mult), "Ve")
                    p.V(_)

                    def _(sy, h=h, i0=i0):
                        p.w(sy, "Ve")
                        p.i(sy.dma_start(out=att[h * 64:h * 64 + 64, i0:i0 + CH],
                                         in_=atf[0:64, 0:CH]), "Ou", 16)
                    p.S(_)

        def _(sy):
            p.w(sy, "Ou")
        p.S(_)
    return nc


def build_launch_ca(ntok, trunc=99, tstage=9, fz=None, pre="ca_"):
    NSH = ntok + 512
    NTT = NSH // 128
    CH = 512
    def mk(nc):
        return {
            "xin": nc.dram_tensor("xin", [D, ntok + 513], F32, kind="ExternalInput").ap(),
            "wqkv": nc.dram_tensor("wqkv", [D, 3 * INNER], F32, kind="ExternalInput").ap(),
            "wo": nc.dram_tensor("wo", [INNER, D], F32, kind="ExternalInput").ap(),
            "tblq": nc.dram_tensor("tblq", [NSH, 2048], F32, kind="ExternalInput").ap(),
            "tblk": nc.dram_tensor("tblk", [NSH, 2048], F32, kind="ExternalInput").ap(),
            "qsks": nc.dram_tensor("qsks", [1, 128], F32, kind="ExternalInput").ap(),
            "mskw": nc.dram_tensor("mskw", [128, 3 * 384], F32, kind="ExternalInput").ap(),
            "iden": nc.dram_tensor("iden", [128, 128], F32, kind="ExternalInput").ap(),
            "out": nc.dram_tensor("out", [D, ntok], F32, kind="ExternalOutput").ap(),
        }
    nc, p0, px, io = _stage(fz, pre, ['xin', 'wqkv', 'wo', 'tblq', 'tblk', 'qsks', 'mskw', 'iden', 'out'], mk)
    xin, wqkv, wo, tblq, tblk, qsks, mskw, iden, out = io["xin"], io["wqkv"], io["wo"], io["tblq"], io["tblk"], io["qsks"], io["mskw"], io["iden"], io["out"]
    with ExitStack() as st:
        sb = lambda n, sh, d: st.enter_context(nc.sbuf_tensor(px + n, sh, d))

        def pst(n, sh, d=F32):
            return st.enter_context(nc.psum_tensor(px + n, sh, d))
        id_s = sb("id_s", [128, 128], BF16)
        ones64 = sb("ones64", [128, 64], F32)
        onesr = sb("onesr", [1, 128], F32)
        wqkv_s = sb("wqkv_s", [128, NCT * 3 * INNER], BF16)
        wo_s = sb("wo_s", [128, NCT * D], BF16)
        qsks_s = sb("qsks_s", [1, 128], F32)
        qsr = sb("qsr", [128, 4096], F32)
        xs = sb("xs", [128, NCT * 513], F32)
        xshb = sb("xshb", [128, NCT * CH], BF16)
        tbq = sb("tbq", [128, 2048], F32)
        tbk = tbq
        q_tm = sb("q_tm", [128, 1024], F32)
        k_tm = sb("k_tm", [128, 1024], F32)
        sqb = sb("sqb", [128, 1024], F32)
        rotA = sb("rotA", [128, 1024], F32)
        nrm = sb("nrm", [128, 64], F32)
        rrep = sb("rrep", [128, 2048], F32)
        qbf = sb("qbf", [128, 1024], BF16)
        kbf = sb("kbf", [128, 1024], BF16)
        qcm = sb("qcm", [128, 4 * 1024], BF16)
        kcm = sb("kcm", [128, 8 * 1024], BF16)
        vtm = sb("vtm", [128, 8 * 1024], BF16)
        ptm = sb("ptm", [128, 384], BF16)
        ptt = sb("ptt", [128, 384], BF16)
        nmax = sb("nmax", [128, 2], F32)
        dnm = sb("dnm", [128, 2], F32)
        msk_s = sb("msk_s", [128, 3 * 384], BF16)
        att_all = sb("att_all", [128, NCT * CH], BF16)
        xm_o = sb("xm_o", [128, CH], F32)
        ps_a = pst("ps_a", [128, 512])
        ps_b = pst("ps_b", [128, 512])
        ps_sc = pst("ps_sc", [128, 384])
        ps_pv = pst("ps_pv", [128, 512])
        ps_tr = pst("ps_tr", [128, 1024], BF16)
        p = p0 or P(nc, st)
        if p0 is None:
            cst, emit_csts = setup_consts(nc, st, p)
            emit_csts(p)

        def _(v):
            v.memset(ones64[:, :], 1.0)
            p.i(v.memset(onesr[:, :], 1.0), "Ve")
        p.V(_)

        def _(g):
            p.i(g.dma_start(out=id_s[:, :], in_=iden), "Lg", 16)
            p.i(g.dma_start(out=wqkv_s[:, :].rearrange("p (c n) -> p c n", n=3 * INNER),
                            in_=cm(wqkv)), "Lg", 16)
            p.i(g.dma_start(out=wo_s[:, :].rearrange("p (c n) -> p c n", n=D),
                            in_=cm(wo)), "Lg", 16)
        p.G(_)

        def _(sy):
            p.i(sy.dma_start(out=qsks_s[:, :], in_=qsks), "Ld", 16)
        p.S(_)

        def _(g):
            p.i(g.dma_start(out=msk_s[:, :], in_=mskw), "Lg", 16)
        p.G(_)

        def _(t):
            p.w(t, "Ld", "Ve")
            t.matmul(ps_a[:, 0:64], onesr[0:1, 0:128], qsks_s[0:1, 0:64],
                     start=True, stop=True)
            p.i(t.matmul(ps_a[:, 64:128], onesr[0:1, 0:128], qsks_s[0:1, 64:128],
                         start=True, stop=True), "Tm")
        p.T(_)

        def _(v):
            p.w(v, "Tm")
            for h in range(16):
                o = h * 64
                v.tensor_copy(qsr[:, o:o + 64], ps_a[:, 0:64])
                v.tensor_copy(qsr[:, 1024 + o:1024 + o + 32], ps_a[:, 32:64])
                v.tensor_copy(qsr[:, 1024 + o + 32:1024 + o + 64], ps_a[:, 0:32])
                v.tensor_copy(qsr[:, 2048 + o:2048 + o + 64], ps_a[:, 64:128])
                v.tensor_copy(qsr[:, 3072 + o:3072 + o + 32], ps_a[:, 96:128])
                v.tensor_copy(qsr[:, 3072 + o + 32:3072 + o + 64], ps_a[:, 64:96])
            p.i(p.vf(v), "Ve")
        p.V(_)


        def _stub():
            def _(v):
                p.i(v.memset(xm_o[:, :], 0.0), "Ve")
            p.V(_)

            def _(sy):
                p.w(sy, "Ve")
                for dt in range(NCT):
                    for cj in range(ntok // CH):
                        p.i(sy.dma_start(out=out[dt * 128:(dt + 1) * 128,
                                                 cj * CH:(cj + 1) * CH],
                                         in_=xm_o[:, :]), "Ou", 16)
                p.w(sy, "Ou")
            p.S(_)

        if trunc <= 1:
            _stub()
            return nc

        nchunks = NSH // CH
        for ci in range(nchunks):
            if trunc <= 2 + ci:
                _stub()
                return nc
            s0 = ci * CH
            halo_only = (ci == 0)

            def _(sy, s0=s0):
                p.w(sy, "Ve", "Ou", "Tm")
                p.i(sy.dma_start(out=xs[:, :].rearrange("p (c n) -> p c n", n=513),
                                 in_=cm(xin)[:, :, s0:s0 + 513]), "Ld", 16)
            p.S(_)

            def _(v):
                p.w(v, "Ld")
                for ct in range(NCT):
                    o = ct * 513
                    off = 1 if ct < 4 else 0
                    v.tensor_copy(xshb[:, ct * CH:(ct + 1) * CH], xs[:, o + off:o + off + CH])
                p.i(p.vf(v), "Ve")
            p.V(_)

            for tt4 in range(4):
                tt = ci * 4 + tt4
                i0l = tt4 * 128

                for gi in range(6 if tstage >= 1 else 0):
                    pq = ps_a if gi % 2 == 0 else ps_b

                    def _(t, gi=gi, i0l=i0l, pq=pq):
                        p.w(t, "Ve", "Se")
                        last = None
                        for ct in range(NCT):
                            last = t.matmul(pq[:, 0:512],
                                            xshb[:, ct * CH + i0l: ct * CH + i0l + 128],
                                            wqkv_s[:, ct * 3 * INNER + gi * 512:
                                                   ct * 3 * INNER + gi * 512 + 512],
                                            start=(ct == 0), stop=(ct == NCT - 1))
                        p.i(last, "Tm")
                    p.T(_)

                    if gi < 4:
                        def _(v, gi=gi, pq=pq):
                            p.w(v, "Tm")
                            dst = q_tm if gi < 2 else k_tm
                            o = (gi % 2) * 512
                            p.i(v.tensor_copy(dst[:, o:o + 512], pq[:, 0:512]), "Ve")
                        p.V(_)
                    else:
                        def _(v, gi=gi, pq=pq, tt=tt):
                            p.w(v, "Tm")
                            o = (tt % 8) * 1024 + (gi - 4) * 512
                            p.i(v.tensor_copy(vtm[:, o:o + 512], pq[:, 0:512]), "Ve")
                        p.V(_)

                if tstage < 2:
                    continue
                def _(v):
                    p.w(v, "Se")
                    v.tensor_tensor(sqb[:, :], q_tm[:, :], q_tm[:, :], ALU.mult)
                    p.i(v.tensor_tensor(rotA[:, :], k_tm[:, :], k_tm[:, :], ALU.mult),
                        "Ve")
                p.V(_)

                def _(v):
                    p.w(v, "Se")
                    v.tensor_reduce(nrm[:, 0:16],
                                    sqb[:, :].rearrange("p (h d) -> p h d", d=64),
                                    AX.X, ALU.add)
                    v.tensor_reduce(nrm[:, 16:32],
                                    rotA[:, :].rearrange("p (h d) -> p h d", d=64),
                                    AX.X, ALU.add)
                    p.i(p.vf(v), "Ve")
                p.V(_)

                def _(s):
                    p.w(s, "Ve")
                    s.activation(nrm[:, 32:64], nrm[:, 0:32], AF.Ln, bias=1e-38)
                    p.sf(s)
                    s.activation(nrm[:, 32:64], nrm[:, 32:64], AF.Exp, scale=-0.5)
                    p.i(p.sf(s), "Se")
                p.A(_)

                if tstage < 3:
                    continue
                def _(v):
                    p.w(v, "Se")
                    v.tensor_scalar(nrm[:, 32:64], nrm[:, 32:64], 1e12, None, ALU.min)
                    p.vf(v)
                    for h in range(16):
                        v.tensor_scalar(rrep[:, h * 64:h * 64 + 64], ones64[:, :],
                                        nrm[:, 32 + h:33 + h], 8.0, ALU.mult, ALU.mult)
                        v.tensor_scalar(rrep[:, 1024 + h * 64:1024 + h * 64 + 64],
                                        ones64[:, :], nrm[:, 48 + h:49 + h], None,
                                        ALU.mult)
                    p.i(p.vf(v), "Ve")
                p.V(_)

                if tstage < 4:
                    continue
                for (src, tblx, qo, ro, dst) in ((q_tm, tblq, 0, 0, qbf),
                                                 (k_tm, tblk, 2048, 1024, kbf)):
                    def _(sy, tt=tt, tblx=tblx):
                        p.w(sy, "Ve")
                        p.i(sy.dma_start(out=tbq[:, :],
                                         in_=tblx[tt * 128:(tt + 1) * 128, :]), "Ld", 16)
                    p.S(_)

                    def _(v, qo=qo):
                        p.w(v, "Ld")
                        v.tensor_tensor(tbq[:, 0:1024], tbq[:, 0:1024],
                                        qsr[:, qo:qo + 1024], ALU.mult)
                        p.i(v.tensor_tensor(tbq[:, 1024:2048], tbq[:, 1024:2048],
                                            qsr[:, qo + 1024:qo + 2048], ALU.mult), "Ve")
                    p.V(_)

                    def _(v, src=src, ro=ro, dst=dst):
                        s3 = src[:, :].rearrange("p (h d) -> p h d", d=64)
                        b3 = sqb[:, :].rearrange("p (h d) -> p h d", d=64)
                        t3 = tbq[:, 1024:2048].rearrange("p (h d) -> p h d", d=64)
                        v.tensor_tensor(b3[:, :, 0:32], s3[:, :, 32:64], t3[:, :, 0:32],
                                        ALU.mult)
                        v.tensor_tensor(b3[:, :, 32:64], s3[:, :, 0:32], t3[:, :, 32:64],
                                        ALU.mult)
                        v.tensor_tensor(rotA[:, :], src[:, :], tbq[:, 0:1024], ALU.mult)
                        v.tensor_tensor(rotA[:, :], rotA[:, :], sqb[:, :], ALU.add)
                        p.i(v.tensor_tensor(dst[:, :], rotA[:, :],
                                            rrep[:, ro:ro + 1024], ALU.mult), "Ve")
                    p.V(_)

                if tstage < 5:
                    continue
                def _(t, tt4=tt4, tt=tt):
                    p.w(t, "Ve")
                    last = None
                    for et in range(8):
                        last = t.transpose(ps_tr[:, et * 128:(et + 1) * 128],
                                           qbf[:, et * 128:(et + 1) * 128], id_s[:, :])
                    p.i(last, "Tm")
                p.T(_)

                def _(v, tt4=tt4):
                    p.w(v, "Tm")
                    p.i(v.tensor_copy(qcm[:, tt4 * 1024:(tt4 + 1) * 1024], ps_tr[:, :]),
                        "Ve")
                p.V(_)

                if tstage < 6:
                    continue

                def _(t, tt=tt):
                    p.w(t, "Ve")
                    last = None
                    for et in range(8):
                        last = t.transpose(ps_tr[:, et * 128:(et + 1) * 128],
                                           kbf[:, et * 128:(et + 1) * 128], id_s[:, :])
                    p.i(last, "Tm")
                p.T(_)

                def _(v, tt=tt):
                    p.w(v, "Tm")
                    p.i(v.tensor_copy(kcm[:, (tt % 8) * 1024:((tt % 8) + 1) * 1024],
                                      ps_tr[:, :]), "Ve")
                p.V(_)

            if halo_only:
                continue

            for tt4 in range(4):
                it = ci * 4 + tt4
                mv = 1 if it == 4 else (2 if it == 5 else 0)
                for h in range(16):
                    psl = slice((h % 2) * 64, (h % 2) * 64 + 64)
                    et = h // 2

                    def _(t, tt4=tt4, it=it, psl=psl, et=et):
                        p.w(t, "Ve", "Se")
                        last = None
                        for jr in range(3):
                            jt = it - 2 + jr
                            last = t.matmul(
                                ps_sc[:, jr * 128:(jr + 1) * 128],
                                qcm[psl, tt4 * 1024 + et * 128: tt4 * 1024 + et * 128 + 128],
                                kcm[psl, (jt % 8) * 1024 + et * 128:
                                    (jt % 8) * 1024 + et * 128 + 128],
                                start=True, stop=True)
                        p.i(last, "Tm")
                    p.T(_)

                    def _(v, mv=mv):
                        p.w(v, "Tm")
                        v.tensor_tensor(ps_sc[:, :], ps_sc[:, :],
                                        msk_s[:, mv * 384:(mv + 1) * 384], ALU.add)
                        v.reduce_max(nmax[:, 0:1], ps_sc[:, :], AX.X, negate=True)
                        p.i(p.vf(v), "Ve")
                    p.V(_)

                    def _(s):
                        p.w(s, "Ve")
                        s.activation(ptm[:, :], ps_sc[:, :], AF.Exp, bias=nmax[:, 0:1],
                                     accum_out=dnm[:, 0:1])
                        p.i(p.sf(s), "Se")
                    p.A(_)

                    def _(v):
                        p.w(v, "Se")
                        v.reciprocal(dnm[:, 1:2], dnm[:, 0:1])
                        p.vf(v)
                        p.i(v.tensor_scalar(ptm[:, :], ptm[:, :], dnm[:, 1:2], None,
                                            ALU.mult), "Ve")
                    p.V(_)

                    def _(t):
                        p.w(t, "Ve")
                        last = None
                        for jr in range(3):
                            last = t.transpose(ps_tr[:, jr * 128:(jr + 1) * 128],
                                               ptm[:, jr * 128:(jr + 1) * 128], id_s[:, :])
                        p.i(last, "Tm")
                    p.T(_)

                    def _(v):
                        p.w(v, "Tm")
                        p.i(v.tensor_copy(ptt[:, :], ps_tr[:, 0:384]), "Ve")
                    p.V(_)

                    def _(t, it=it, h=h):
                        p.w(t, "Ve")
                        last = None
                        for jr in range(3):
                            jt = it - 2 + jr
                            last = t.matmul(ps_pv[0:64, 0:128],
                                            vtm[:, (jt % 8) * 1024 + h * 64:
                                                (jt % 8) * 1024 + h * 64 + 64],
                                            ptt[:, jr * 128:(jr + 1) * 128],
                                            start=(jr == 0), stop=(jr == 2))
                        p.i(last, "Tm")
                    p.T(_)

                    def _(v, tt4=tt4, h=h, psl=psl, et=et):
                        p.w(v, "Tm")
                        p.i(v.tensor_copy(att_all[psl, et * CH + tt4 * 128:
                                                  et * CH + tt4 * 128 + 128],
                                          ps_pv[0:64, 0:128]), "Ve")
                    p.V(_)

            for dt in range(NCT):
                pd = ps_a if dt % 2 == 0 else ps_b

                def _(t, dt=dt, pd=pd):
                    p.w(t, "Se", "Ve")
                    last = None
                    for et in range(NCT):
                        last = t.matmul(pd[:, 0:CH],
                                        wo_s[:, et * D + dt * 128: et * D + dt * 128 + 128],
                                        att_all[:, et * CH:(et + 1) * CH],
                                        start=(et == 0), stop=(et == NCT - 1))
                    p.i(last, "Tm")
                p.T(_)

                def _(v, dt=dt, pd=pd):
                    p.w(v, "Tm", "Ou")
                    p.i(v.tensor_tensor(xm_o[:, :], pd[:, 0:CH],
                                        xs[:, dt * 513 + 1: dt * 513 + 1 + CH], ALU.add),
                        "Ve")
                p.V(_)

                def _(sy, dt=dt, ci=ci):
                    p.w(sy, "Ve")
                    p.i(sy.dma_start(out=out[dt * 128:(dt + 1) * 128, (ci - 1) * CH: ci * CH],
                                     in_=xm_o[:, :]), "Ou", 16)
                p.S(_)

        def _(sy):
            p.w(sy, "Ou")
        p.S(_)
    return nc


def build_launch_cb(ntok, final, fz=None, pre="cb_"):
    CH = 256
    nch = ntok // CH
    NDT = 2 * DIP // 128
    NHT = DIP // 128
    def mk(nc):
        return {
            "xin": nc.dram_tensor("xin", [D, ntok + 1], F32, kind="ExternalInput").ap(),
            "w1p": nc.dram_tensor("w1p", [D, 2 * DIP], F32, kind="ExternalInput").ap(),
            "b1p": nc.dram_tensor("b1p", [128, NDT], F32, kind="ExternalInput").ap(),
            "w2p": nc.dram_tensor("w2p", [DIP, D], F32, kind="ExternalInput").ap(),
            "lgb": nc.dram_tensor("lgb", [D, 3], F32, kind="ExternalInput").ap(),
            "x1": nc.dram_tensor("x1", [D, ntok], F32, kind="ExternalInput").ap() if final else None,
            "out": nc.dram_tensor("out", [D, ntok], F32, kind="ExternalOutput").ap(),
        }
    nc, p0, px, io = _stage(fz, pre, ['xin', 'w1p', 'b1p', 'w2p', 'lgb', 'x1', 'out'], mk)
    xin, w1p, b1p, w2p, lgb, x1, out = io["xin"], io["w1p"], io["b1p"], io["w2p"], io["lgb"], io["x1"], io["out"]
    with ExitStack() as st:
        sb = lambda n, sh, d: st.enter_context(nc.sbuf_tensor(px + n, sh, d))

        def pst(n, sh, d=F32):
            return st.enter_context(nc.psum_tensor(px + n, sh, d))
        onec = sb("onec", [128, 1], BF16)
        onesr = sb("onesr", [1, 128], F32)
        w1_s = sb("w1_s", [128, NCT * 2 * DIP], BF16)
        w2_s = sb("w2_s", [128, NHT * D], BF16)
        b1_s = sb("b1_s", [128, NDT], F32)
        b1r = sb("b1r", [128, NHT], F32)
        lgb_s = sb("lgb_s", [128, NCT * 3], F32)
        xs = sb("xs", [128, NCT * (CH + 1)], F32)
        xsh = sb("xsh", [128, NCT * CH], BF16)
        xsq = sb("xsq", [128, NCT * CH], BF16)
        xnb = sb("xnb", [128, NCT * CH], BF16)
        abf = sb("abf", [128, CH], BF16)
        gbf = sb("gbf", [128, CH], BF16)
        erf = sb("erf", [128, CH], F32)
        t3 = sb("t3", [128, CH], F32)
        t4 = sb("t4", [128, CH], F32)
        hb = sb("hb", [128, NHT * CH], BF16)
        yc = sb("yc", [128, NCT * CH], F32)
        tmpf = sb("tmpf", [128, CH], F32)
        stat = sb("stat", [1, 4 * CH], F32)
        bc_mu = sb("bc_mu", [128, CH], F32)
        bc_is = sb("bc_is", [128, CH], F32)
        if final:
            x1c = sb("x1c", [128, NCT * CH], F32)
            x6b = sb("x6b", [128, NCT * CH], BF16)
            x6q = sb("x6q", [128, NCT * CH], BF16)
        ps_a = pst("ps_a", [128, 512])
        ps_b = pst("ps_b", [128, 512])
        ps_s1 = pst("ps_s1", [1, 512])
        ps_s2 = pst("ps_s2", [1, 512])
        ps_bc = pst("ps_bc", [128, 512])
        p = p0 or P(nc, st)
        if p0 is None:
            cst, emit_csts = setup_consts(nc, st, p)
            emit_csts(p)

        def _(v):
            v.memset(onec[:, :], 1.0)
            p.i(v.memset(onesr[:, :], 1.0), "Ve")
        p.V(_)

        def _(g):
            p.i(g.dma_start(out=w1_s[:, :].rearrange("p (c n) -> p c n", n=2 * DIP),
                            in_=cm(w1p)), "Lg", 16)
            p.i(g.dma_start(out=w2_s[:, :].rearrange("p (c n) -> p c n", n=D),
                            in_=cm(w2p)), "Lg", 16)
        p.G(_)

        def _(sy):
            p.i(sy.dma_start(out=b1_s[:, :], in_=b1p), "Ld", 16)
            p.i(sy.dma_start(out=lgb_s[:, :].rearrange("p (c n) -> p c n", n=3),
                             in_=cm(lgb)), "Ld", 16)
        p.S(_)

        def _(v):
            p.w(v, "Ld")
            p.i(v.tensor_scalar(b1r[:, :], b1_s[:, NHT:NDT], RS2, None, ALU.mult), "Ve")
        p.V(_)

        for ci in range(nch):
            j0 = ci * CH

            def _(sy, j0=j0):
                p.w(sy, "Ve", "Ou", "Se")
                p.i(sy.dma_start(out=xs[:, :].rearrange("p (c n) -> p c n", n=CH + 1),
                                 in_=cm(xin)[:, :, j0:j0 + CH + 1]), "Ld", 16)
            p.S(_)

            if final:
                def _(g, j0=j0):
                    p.w(g, "Ve", "Ou", "Se")
                    p.i(g.dma_start(out=x1c[:, :].rearrange("p (c n) -> p c n", n=CH),
                                    in_=cm(x1)[:, :, j0:j0 + CH]), "Lg", 16)
                p.G(_)

            def _(v):
                p.w(v, "Ld")
                for ct in range(NCT):
                    o = ct * (CH + 1)
                    off = 1 if ct < 4 else 0
                    v.tensor_copy(xsh[:, ct * CH:(ct + 1) * CH], xs[:, o + off:o + off + CH])
                p.i(p.vf(v), "Ve")
            p.V(_)

            def _(v):
                p.i(v.tensor_tensor(xsq[:, :], xsh[:, :], xsh[:, :], ALU.mult), "Ve")
            p.V(_)

            emit_stats_mm(p, ps_s1, ps_s2, onec,
                          lambda ct: xsh[:, ct * CH:(ct + 1) * CH],
                          lambda ct: xsq[:, ct * CH:(ct + 1) * CH], CH)
            emit_lnpair(p, ps_s1, ps_s2, stat, CH)
            emit_bcast2(p, ps_bc, onesr, stat, bc_mu, bc_is, CH)

            def _(v):
                for ct in range(NCT):
                    o = ct * (CH + 1)
                    off = 1 if ct < 4 else 0
                    v.tensor_tensor(tmpf[:, :], xs[:, o + off:o + off + CH], bc_mu[:, :],
                                    ALU.subtract)
                    v.tensor_tensor(tmpf[:, :], tmpf[:, :], bc_is[:, :], ALU.mult)
                    v.tensor_scalar(xnb[:, ct * CH:(ct + 1) * CH], tmpf[:, :],
                                    lgb_s[:, ct * 3:ct * 3 + 1], lgb_s[:, ct * 3 + 1:ct * 3 + 2],
                                    ALU.mult, ALU.add)
                p.i(p.vf(v), "Ve")
            p.V(_)

            for dtp in range(NHT):
                def _(t, dtp=dtp):
                    p.w(t, "Ve", "Se")
                    for ct in range(NCT):
                        t.matmul(ps_a[:, 0:CH],
                                 w1_s[:, ct * 2 * DIP + dtp * 128: ct * 2 * DIP + dtp * 128 + 128],
                                 xnb[:, ct * CH:(ct + 1) * CH],
                                 start=(ct == 0), stop=(ct == NCT - 1))
                    last = None
                    for ct in range(NCT):
                        last = t.matmul(
                            ps_b[:, 0:CH],
                            w1_s[:, ct * 2 * DIP + (NHT + dtp) * 128:
                                 ct * 2 * DIP + (NHT + dtp) * 128 + 128],
                            xnb[:, ct * CH:(ct + 1) * CH],
                            start=(ct == 0), stop=(ct == NCT - 1))
                    p.i(last, "Tm")
                p.T(_)

                def _(s, dtp=dtp):
                    p.w(s, "Tm")
                    p.i(s.activation(erf[:, :], ps_b[:, 0:CH], AF.Erf, scale=RS2,
                                     bias=b1r[:, dtp:dtp + 1]), "Se")
                p.A(_)

                def _(v, dtp=dtp):
                    p.w(v, "Se")
                    v.tensor_scalar(abf[:, :], ps_a[:, 0:CH], b1_s[:, dtp:dtp + 1], None,
                                    ALU.add)
                    v.tensor_scalar(gbf[:, :], ps_b[:, 0:CH], b1_s[:, NHT + dtp:NHT + dtp + 1],
                                    None, ALU.add)
                    v.tensor_scalar(t3[:, :], erf[:, :], 0.5, 0.5, ALU.mult, ALU.add)
                    v.tensor_tensor(t4[:, :], abf[:, :], gbf[:, :], ALU.mult)
                    p.i(v.tensor_tensor(hb[:, dtp * CH:(dtp + 1) * CH], t4[:, :], t3[:, :],
                                        ALU.mult), "Ve")
                p.V(_)

            for dt2 in range(NCT):
                pd = ps_a if dt2 % 2 == 0 else ps_b

                def _(t, dt2=dt2, pd=pd):
                    p.w(t, "Ve", "Se")
                    last = None
                    for ht in range(NHT):
                        last = t.matmul(pd[:, 0:CH],
                                        w2_s[:, ht * D + dt2 * 128: ht * D + dt2 * 128 + 128],
                                        hb[:, ht * CH:(ht + 1) * CH],
                                        start=(ht == 0), stop=(ht == NHT - 1))
                    p.i(last, "Tm")
                p.T(_)

                def _(v, dt2=dt2, pd=pd):
                    p.w(v, "Tm")
                    sl = slice(dt2 * CH, (dt2 + 1) * CH)
                    o = dt2 * (CH + 1)
                    v.tensor_scalar(tmpf[:, :], pd[:, 0:CH],
                                    lgb_s[:, dt2 * 3 + 2:dt2 * 3 + 3], None, ALU.add)
                    p.i(v.tensor_tensor(yc[:, sl], tmpf[:, :], xs[:, o + 1:o + 1 + CH],
                                        ALU.add), "Ve")
                p.V(_)

            if not final:
                def _(sy, j0=j0):
                    p.w(sy, "Ve")
                    p.i(sy.dma_start(out=cm(out)[:, :, j0:j0 + CH],
                                     in_=yc[:, :].rearrange("p (c n) -> p c n", n=CH)),
                        "Ou", 16)
                p.S(_)
            else:
                def _(s):
                    p.w(s, "Ve")
                    p.i(s.activation(x6b[:, :], yc[:, :], AF.Copy), "Se")
                p.A(_)

                def _(v):
                    p.w(v, "Se")
                    p.i(v.tensor_tensor(x6q[:, :], x6b[:, :], x6b[:, :], ALU.mult), "Ve")
                p.V(_)

                emit_stats_mm(p, ps_s1, ps_s2, onec,
                              lambda ct: x6b[:, ct * CH:(ct + 1) * CH],
                              lambda ct: x6q[:, ct * CH:(ct + 1) * CH], CH, deps=("Se",))
                emit_lnpair(p, ps_s1, ps_s2, stat, CH)
                emit_bcast2(p, ps_bc, onesr, stat, bc_mu, bc_is, CH)

                def _(v):
                    for ct in range(NCT):
                        sl = slice(ct * CH, (ct + 1) * CH)
                        v.tensor_tensor(tmpf[:, :], yc[:, sl], bc_mu[:, :], ALU.subtract)
                        v.tensor_tensor(tmpf[:, :], tmpf[:, :], bc_is[:, :], ALU.mult)
                        v.tensor_tensor(yc[:, sl], tmpf[:, :], x1c[:, sl], ALU.add)
                    p.i(p.vf(v), "Ve")
                p.V(_)

                def _(sy, j0=j0):
                    p.w(sy, "Ve")
                    p.i(sy.dma_start(out=cm(out)[:, :, j0:j0 + CH],
                                     in_=yc[:, :].rearrange("p (c n) -> p c n", n=CH)),
                        "Ou", 16)
                p.S(_)

        def _(sy):
            p.w(sy, "Ou")
        p.S(_)
    return nc


def build_launch_ln(ntok, fz=None, pre="h_"):
    CH = 512
    nch = ntok // CH
    def mk(nc):
        return {
            "xin": nc.dram_tensor("xin", [D, ntok], F32, kind="ExternalInput").ap(),
            "out": nc.dram_tensor("out", [D, ntok], F32, kind="ExternalOutput").ap(),
        }
    nc, p0, px, io = _stage(fz, pre, ['xin', 'out'], mk)
    xin, out = io["xin"], io["out"]
    with ExitStack() as st:
        sb = lambda n, s, d: st.enter_context(nc.sbuf_tensor(px + n, s, d))
        pst = lambda n, s: st.enter_context(nc.psum_tensor(px + n, s, F32))
        onec = sb("onec", [128, 1], BF16)
        onesr = sb("onesr", [1, 128], F32)
        xc = sb("xc", [128, NCT * CH], F32)
        xb = sb("xb", [128, NCT * CH], BF16)
        xsq = sb("xsq", [128, NCT * CH], BF16)
        oc = sb("oc", [128, NCT * CH], F32)
        tmpf = sb("tmpf", [128, CH], F32)
        stat = sb("stat", [1, 4 * CH], F32)
        bc_mu = sb("bc_mu", [128, CH], F32)
        bc_is = sb("bc_is", [128, CH], F32)
        ps_s1 = pst("ps_s1", [1, 512])
        ps_s2 = pst("ps_s2", [1, 512])
        ps_bc = pst("ps_bc", [128, 512])
        p = p0 or P(nc, st)
        if p0 is None:
            cst, emit_csts = setup_consts(nc, st, p)
            emit_csts(p)

        def _(v):
            v.memset(onec[:, :], 1.0)
            p.i(v.memset(onesr[:, :], 1.0), "Ve")
        p.V(_)

        for ci in range(nch):
            j0 = ci * CH

            def _(sy, j0=j0):
                p.w(sy, "Ve", "Ou")
                p.i(sy.dma_start(out=xc[:, :].rearrange("p (c n) -> p c n", n=CH),
                                 in_=cm(xin)[:, :, j0:j0 + CH]), "Ld", 16)
            p.S(_)

            def _(g, j0=j0):
                p.w(g, "Ve", "Ou")
                p.i(g.dma_start(out=xb[:, :].rearrange("p (c n) -> p c n", n=CH),
                                in_=cm(xin)[:, :, j0:j0 + CH]), "Lg", 16)
            p.G(_)

            def _(v):
                p.w(v, "Ld")
                p.i(v.tensor_tensor(xsq[:, :], xb[:, :], xb[:, :], ALU.mult), "Ve")
            p.V(_)

            emit_stats_mm(p, ps_s1, ps_s2, onec,
                          lambda ct: xb[:, ct * CH:(ct + 1) * CH],
                          lambda ct: xsq[:, ct * CH:(ct + 1) * CH], CH)
            emit_lnpair(p, ps_s1, ps_s2, stat, CH)
            emit_bcast2(p, ps_bc, onesr, stat, bc_mu, bc_is, CH)

            def _(v):
                last = None
                for ct in range(NCT):
                    sl = slice(ct * CH, (ct + 1) * CH)
                    v.tensor_tensor(tmpf[:, :], xc[:, sl], bc_mu[:, :], ALU.subtract)
                    last = v.tensor_tensor(oc[:, sl], tmpf[:, :], bc_is[:, :], ALU.mult)
                p.i(last, "Ve")
            p.V(_)

            def _(sy, j0=j0):
                p.w(sy, "Ve")
                p.i(sy.dma_start(out=cm(out)[:, :, j0:j0 + CH],
                                 in_=oc[:, :].rearrange("p (c n) -> p c n", n=CH)), "Ou", 16)
            p.S(_)

        def _(sy):
            p.w(sy, "Ou")
        p.S(_)
    return nc


def build_launch_b(ntok, fz=None, pre="b_"):
    CH = 512
    nch = ntok // CH
    CH2 = CH + 2
    def mk(nc):
        return {
            "xin": nc.dram_tensor("xin", [D, ntok + 2], F32, kind="ExternalInput").ap(),
            "att": nc.dram_tensor("att", [D, ntok], F32, kind="ExternalInput").ap(),
            "wpT": nc.dram_tensor("wpT", [D, D], F32, kind="ExternalInput").ap(),
            "cwb": nc.dram_tensor("cwb", [D, 5], F32, kind="ExternalInput").ap(),
            "out": nc.dram_tensor("out", [D, ntok], F32, kind="ExternalOutput").ap(),
        }
    nc, p0, px, io = _stage(fz, pre, ['xin', 'att', 'wpT', 'cwb', 'out'], mk)
    xin, att, wpT, cwb, out = io["xin"], io["att"], io["wpT"], io["cwb"], io["out"]
    with ExitStack() as st:
        sb = lambda n, s, d: st.enter_context(nc.sbuf_tensor(px + n, s, d))
        pst = lambda n, s: st.enter_context(nc.psum_tensor(px + n, s, F32))
        onec = sb("onec", [128, 1], BF16)
        onesr = sb("onesr", [1, 128], F32)
        wp_s = sb("wp_s", [128, NCT * D], BF16)
        cw_s = sb("cw_s", [128, NCT * 5], F32)
        xc = sb("xc", [128, NCT * CH2], F32)
        xb = sb("xb", [128, NCT * CH2], BF16)
        xsq = sb("xsq", [128, NCT * CH2], BF16)
        xn = sb("xn", [128, NCT * CH2], F32)
        atf = sb("atf", [128, NCT * CH], F32)
        sbf = sb("sbf", [128, NCT * CH], BF16)
        yc = sb("yc", [128, NCT * CH], F32)
        yb = xb
        ysq = xsq
        tmpf = sb("tmpf", [128, CH2], F32)
        tmp2 = sb("tmp2", [128, CH2], F32)
        stat = sb("stat", [1, 4 * CH], F32)
        bc_mu = sb("bc_mu", [128, CH], F32)
        bc_is = sb("bc_is", [128, CH], F32)
        ps_s1 = pst("ps_s1", [1, 512])
        ps_s2 = pst("ps_s2", [1, 512])
        ps_bc = pst("ps_bc", [128, 512])
        ps_a = pst("ps_a", [128, 512])
        ps_b = pst("ps_b", [128, 512])

        p = p0 or P(nc, st)
        if p0 is None:
            cst, emit_csts = setup_consts(nc, st, p)
            emit_csts(p)

        def _(v):
            v.memset(onec[:, :], 1.0)
            p.i(v.memset(onesr[:, :], 1.0), "Ve")
        p.V(_)

        def _(g):
            p.i(g.dma_start(out=wp_s[:, :].rearrange("p (c n) -> p c n", n=D),
                            in_=cm(wpT)), "Lg", 16)
        p.G(_)

        def _(sy):
            p.i(sy.dma_start(out=cw_s[:, :].rearrange("p (c n) -> p c n", n=5),
                             in_=cm(cwb)), "Ld", 16)
        p.S(_)


        for ci in range(nch):
            j0 = ci * CH

            def _(sy, j0=j0):
                p.w(sy, "Ve", "Ou")
                p.i(sy.dma_start(out=xc[:, :].rearrange("p (c n) -> p c n", n=CH2),
                                 in_=cm(xin)[:, :, j0:j0 + CH2]), "Ld", 16)
                p.i(sy.dma_start(out=atf[:, :].rearrange("p (c n) -> p c n", n=CH),
                                 in_=cm(att)[:, :, j0:j0 + CH]), "Ld", 16)
            p.S(_)

            def _(g, j0=j0):
                p.w(g, "Ve", "Ou")
                p.i(g.dma_start(out=xb[:, :].rearrange("p (c n) -> p c n", n=CH2),
                                in_=cm(xin)[:, :, j0:j0 + CH2]), "Lg", 16)
            p.G(_)

            def _(v):
                p.w(v, "Ld")
                p.i(v.tensor_tensor(xsq[:, :], xb[:, :], xb[:, :], ALU.mult), "Ve")
            p.V(_)

            if ci == 0:
                emit_stats_mm(p, ps_s1, ps_s2, onec,
                              lambda ct: xb[:, ct * CH2: ct * CH2 + CH],
                              lambda ct: xsq[:, ct * CH2: ct * CH2 + CH], CH)
                emit_lnpair(p, ps_s1, ps_s2, stat, CH)
                emit_bcast2(p, ps_bc, onesr, stat, bc_mu, bc_is, CH)

                def _(v):
                    for ct in range(NCT):
                        v.tensor_tensor(tmp2[:, ct:ct + 1], xc[:, ct * CH2:ct * CH2 + 1],
                                        bc_mu[:, 0:1], ALU.subtract)
                    p.vf(v)
                    for ct in range(NCT):
                        v.tensor_tensor(xn[:, ct * CH2:ct * CH2 + 1], tmp2[:, ct:ct + 1],
                                        bc_is[:, 0:1], ALU.mult)
                    p.i(p.vf(v), "Ve")
                p.V(_)
            else:
                def _(v):
                    p.w(v, "Ld")
                    for ct in range(NCT):
                        o = ct * CH2
                        v.tensor_copy(xn[:, o:o + 1], xn[:, o + CH:o + CH + 1])
                    p.i(p.vf(v), "Ve")
                p.V(_)

            emit_stats_mm(p, ps_s1, ps_s2, onec,
                          lambda ct: xb[:, ct * CH2 + 1: ct * CH2 + 1 + CH],
                          lambda ct: xsq[:, ct * CH2 + 1: ct * CH2 + 1 + CH], CH,
                          deps=("Ve", "Se"))
            emit_lnpair(p, ps_s1, ps_s2, stat, CH)
            emit_bcast2(p, ps_bc, onesr, stat, bc_mu, bc_is, CH)

            def _(v):
                for ct in range(NCT):
                    o = ct * CH2
                    v.tensor_tensor(tmpf[:, 0:CH], xc[:, o + 1:o + 1 + CH], bc_mu[:, :],
                                    ALU.subtract)
                    v.tensor_tensor(xn[:, o + 1:o + 1 + CH], tmpf[:, 0:CH], bc_is[:, :],
                                    ALU.mult)
                p.i(p.vf(v), "Ve")
            p.V(_)

            emit_stats_mm(p, ps_s1, ps_s2, onec,
                          lambda ct: xb[:, ct * CH2 + 2: ct * CH2 + 2 + CH],
                          lambda ct: xsq[:, ct * CH2 + 2: ct * CH2 + 2 + CH], CH,
                          deps=("Ve",))
            emit_lnpair(p, ps_s1, ps_s2, stat, CH)
            emit_bcast2(p, ps_bc, onesr, stat, bc_mu, bc_is, CH)

            def _(v):
                for ct in range(NCT):
                    o = ct * CH2
                    v.tensor_tensor(tmp2[:, ct:ct + 1], xc[:, o + CH + 1:o + CH + 2],
                                    bc_mu[:, CH - 1:CH], ALU.subtract)
                p.vf(v)
                for ct in range(NCT):
                    o = ct * CH2
                    v.tensor_tensor(xn[:, o + CH + 1:o + CH + 2], tmp2[:, ct:ct + 1],
                                    bc_is[:, CH - 1:CH], ALU.mult)
                p.i(p.vf(v), "Ve")
            p.V(_)

            def _(v):
                last = None
                for ct in range(NCT):
                    o = ct * CH2
                    v.tensor_scalar(tmpf[:, 0:CH], xn[:, o:o + CH], cw_s[:, ct * 5:ct * 5 + 1],
                                    None, ALU.mult)
                    v.scalar_tensor_tensor(tmpf[:, 0:CH], xn[:, o + 1:o + 1 + CH],
                                           cw_s[:, ct * 5 + 1:ct * 5 + 2], tmpf[:, 0:CH],
                                           ALU.mult, ALU.add)
                    v.scalar_tensor_tensor(tmpf[:, 0:CH], xn[:, o + 2:o + 2 + CH],
                                           cw_s[:, ct * 5 + 2:ct * 5 + 3], tmpf[:, 0:CH],
                                           ALU.mult, ALU.add)
                    v.tensor_scalar(tmp2[:, 0:CH], tmpf[:, 0:CH], cw_s[:, ct * 5 + 3:ct * 5 + 4],
                                    None, ALU.add)
                    last = v.tensor_tensor(sbf[:, ct * CH:(ct + 1) * CH], tmp2[:, 0:CH],
                                           atf[:, ct * CH:(ct + 1) * CH], ALU.add)
                p.i(last, "Ve")
            p.V(_)

            for dt in range(NCT):
                pd = ps_a if dt % 2 == 0 else ps_b

                def _(t, dt=dt, pd=pd):
                    p.w(t, "Ve", "Se")
                    last = None
                    for ct in range(NCT):
                        last = t.matmul(pd[:, 0:CH],
                                        wp_s[:, ct * D + dt * 128: ct * D + dt * 128 + 128],
                                        sbf[:, ct * CH:(ct + 1) * CH],
                                        start=(ct == 0), stop=(ct == NCT - 1))
                    p.i(last, "Tm")
                p.T(_)

                def _(v, dt=dt, pd=pd):
                    p.w(v, "Tm")
                    sl = slice(dt * CH, (dt + 1) * CH)
                    bpv = cw_s[:, dt * 5 + 4:dt * 5 + 5]
                    p.i(v.tensor_scalar(yc[:, sl], pd[:, 0:CH], bpv, None, ALU.add), "Ve")
                p.V(_)

                def _(s, dt=dt):
                    p.w(s, "Ve")
                    sl = slice(dt * CH, (dt + 1) * CH)
                    p.i(s.activation(yb[:, sl], yc[:, sl], AF.Copy), "Se")
                p.A(_)

            def _(v):
                p.w(v, "Se")
                p.i(v.tensor_tensor(ysq[:, 0:NCT * CH], yb[:, 0:NCT * CH],
                                    yb[:, 0:NCT * CH], ALU.mult), "Ve")
            p.V(_)

            emit_stats_mm(p, ps_s1, ps_s2, onec,
                          lambda ct: yb[:, ct * CH:(ct + 1) * CH],
                          lambda ct: ysq[:, ct * CH:(ct + 1) * CH], CH, deps=("Se",))
            emit_lnpair(p, ps_s1, ps_s2, stat, CH)
            emit_bcast2(p, ps_bc, onesr, stat, bc_mu, bc_is, CH)

            def _(v, j0=j0):
                last = None
                for ct in range(NCT):
                    sl = slice(ct * CH, (ct + 1) * CH)
                    o = ct * CH2
                    v.tensor_tensor(tmpf[:, 0:CH], yc[:, sl], bc_mu[:, :], ALU.subtract)
                    v.tensor_tensor(tmpf[:, 0:CH], tmpf[:, 0:CH], bc_is[:, :], ALU.mult)
                    last = v.tensor_tensor(yc[:, sl], tmpf[:, 0:CH], xc[:, o + 1:o + 1 + CH],
                                           ALU.add)
                p.i(last, "Ve")
            p.V(_)

            def _(sy, j0=j0):
                p.w(sy, "Ve")
                p.i(sy.dma_start(out=cm(out)[:, :, j0:j0 + CH],
                                 in_=yc[:, :].rearrange("p (c n) -> p c n", n=CH)), "Ou", 16)
            p.S(_)

        def _(sy):
            p.w(sy, "Ou")
        p.S(_)
    return nc


def build_launch_e(ntok, fz=None, pre="e_"):
    CH = 512
    nch = ntok // CH
    def mk(nc):
        return {
            "att": nc.dram_tensor("att", [D, ntok], F32, kind="ExternalInput").ap(),
            "x2": nc.dram_tensor("x2", [D, ntok], F32, kind="ExternalInput").ap(),
            "woT": nc.dram_tensor("woT", [D, D], F32, kind="ExternalInput").ap(),
            "out": nc.dram_tensor("out", [D, ntok], F32, kind="ExternalOutput").ap(),
        }
    nc, p0, px, io = _stage(fz, pre, ['att', 'x2', 'woT', 'out'], mk)
    att, x2, woT, out = io["att"], io["x2"], io["woT"], io["out"]
    with ExitStack() as st:
        sb = lambda n, s, d: st.enter_context(nc.sbuf_tensor(px + n, s, d))
        pst = lambda n, s: st.enter_context(nc.psum_tensor(px + n, s, F32))
        onec = sb("onec", [128, 1], BF16)
        onesr = sb("onesr", [1, 128], F32)
        wo_s = sb("wo_s", [128, NCT * D], BF16)
        xc = sb("xc", [128, NCT * CH], F32)
        ab = sb("ab", [128, NCT * CH], BF16)
        yc = sb("yc", [128, NCT * CH], F32)
        yb = sb("yb", [128, NCT * CH], BF16)
        ysq = sb("ysq", [128, NCT * CH], BF16)
        tmpf = sb("tmpf", [128, CH], F32)
        stat = sb("stat", [1, 4 * CH], F32)
        bc_mu = sb("bc_mu", [128, CH], F32)
        bc_is = sb("bc_is", [128, CH], F32)
        ps_s1 = pst("ps_s1", [1, 512])
        ps_s2 = pst("ps_s2", [1, 512])
        ps_bc = pst("ps_bc", [128, 512])
        ps_a = pst("ps_a", [128, 512])
        ps_b = pst("ps_b", [128, 512])
        p = p0 or P(nc, st)
        if p0 is None:
            cst, emit_csts = setup_consts(nc, st, p)
            emit_csts(p)

        def _(v):
            v.memset(onec[:, :], 1.0)
            p.i(v.memset(onesr[:, :], 1.0), "Ve")
        p.V(_)

        def _(g):
            p.i(g.dma_start(out=wo_s[:, :].rearrange("p (c n) -> p c n", n=D),
                            in_=cm(woT)), "Lg", 16)
        p.G(_)

        for ci in range(nch):
            j0 = ci * CH

            def _(sy, j0=j0):
                p.w(sy, "Ve", "Ou")
                p.i(sy.dma_start(out=xc[:, :].rearrange("p (c n) -> p c n", n=CH),
                                 in_=cm(x2)[:, :, j0:j0 + CH]), "Ld", 16)
            p.S(_)

            def _(g, j0=j0):
                p.w(g, "Ve", "Ou")
                p.i(g.dma_start(out=ab[:, :].rearrange("p (c n) -> p c n", n=CH),
                                in_=cm(att)[:, :, j0:j0 + CH]), "Lg", 16)
            p.G(_)

            for dt in range(NCT):
                pd = ps_a if dt % 2 == 0 else ps_b

                def _(t, dt=dt, pd=pd):
                    p.w(t, "Ld", "Se")
                    last = None
                    for ct in range(NCT):
                        last = t.matmul(pd[:, 0:CH],
                                        wo_s[:, ct * D + dt * 128: ct * D + dt * 128 + 128],
                                        ab[:, ct * CH:(ct + 1) * CH],
                                        start=(ct == 0), stop=(ct == NCT - 1))
                    p.i(last, "Tm")
                p.T(_)

                def _(s, dt=dt, pd=pd):
                    p.w(s, "Tm")
                    sl = slice(dt * CH, (dt + 1) * CH)
                    s.activation(yc[:, sl], pd[:, 0:CH], AF.Copy)
                    p.i(s.activation(yb[:, sl], pd[:, 0:CH], AF.Copy), "Se")
                p.A(_)

            def _(v):
                p.w(v, "Se")
                p.i(v.tensor_tensor(ysq[:, :], yb[:, :], yb[:, :], ALU.mult), "Ve")
            p.V(_)

            emit_stats_mm(p, ps_s1, ps_s2, onec,
                          lambda ct: yb[:, ct * CH:(ct + 1) * CH],
                          lambda ct: ysq[:, ct * CH:(ct + 1) * CH], CH, deps=("Se",))
            emit_lnpair(p, ps_s1, ps_s2, stat, CH)
            emit_bcast2(p, ps_bc, onesr, stat, bc_mu, bc_is, CH)

            def _(v, j0=j0):
                last = None
                for ct in range(NCT):
                    sl = slice(ct * CH, (ct + 1) * CH)
                    v.tensor_tensor(tmpf[:, :], yc[:, sl], bc_mu[:, :], ALU.subtract)
                    v.tensor_tensor(tmpf[:, :], tmpf[:, :], bc_is[:, :], ALU.mult)
                    last = v.tensor_tensor(yc[:, sl], tmpf[:, :], xc[:, sl], ALU.add)
                p.i(last, "Ve")
            p.V(_)

            def _(sy, j0=j0):
                p.w(sy, "Ve")
                p.i(sy.dma_start(out=cm(out)[:, :, j0:j0 + CH],
                                 in_=yc[:, :].rearrange("p (c n) -> p c n", n=CH)), "Ou", 16)
            p.S(_)

        def _(sy):
            p.w(sy, "Ou")
        p.S(_)
    return nc



def build_fused(nstages=99):
    OFF = 513
    W = OFF + T + 1
    NSH = T + 512
    NDT = 2 * DIP // 128
    nc = bass.Bass("TRN2", target_bir_lowering=False)
    with ExitStack() as st:
        p = P(nc, st)
        cst, emit_csts = setup_consts(nc, st, p)
        emit_csts(p)
        ei = lambda n, sh, dt=F32: nc.dram_tensor(n, sh, dt, kind="ExternalInput").ap()
        io = {}
        x_in = ei("x", [D, T])
        io["a_x"] = x_in
        for nm in ("wqT", "wkT", "wvT"):
            io["a_" + nm] = ei("a_" + nm, [D, 1024])
        io["b_wpT"] = ei("b_wpT", [D, D])
        io["b_cwb"] = ei("b_cwb", [D, 5])
        for li in range(6):
            io[f"ca{li}_wqkv"] = ei(f"ca{li}_wqkv", [D, 3 * INNER])
            io[f"ca{li}_wo"] = ei(f"ca{li}_wo", [INNER, D])
            io[f"ca{li}_qsks"] = ei(f"ca{li}_qsks", [1, 128])
            io[f"cb{li}_w1p"] = ei(f"cb{li}_w1p", [D, 2 * DIP])
            io[f"cb{li}_b1p"] = ei(f"cb{li}_b1p", [128, NDT])
            io[f"cb{li}_w2p"] = ei(f"cb{li}_w2p", [DIP, D])
            io[f"cb{li}_lgb"] = ei(f"cb{li}_lgb", [D, 3])
        tblq = ei("tblq", [NSH, 2048])
        tblk = ei("tblk", [NSH, 2048])
        mskw = ei("mskw", [128, 3 * 384])
        iden = ei("iden", [128, 128])
        for nm in ("wqT", "wkT", "wvT", "woT"):
            io["d_" + nm if nm != "woT" else "e_woT"] = ei("l1_" + nm, [D, D])
        io["d_slp"] = ei("d_slp", [1, 16])
        io["d_pwv"] = ei("d_pwv", [128, 1])
        io["d_bmv"] = ei("d_bmv", [128, 1])
        io["d_pos"] = ei("d_pos", [1, T])
        io["d_msk"] = ei("d_msk", [128, 4 * 512])
        io["d_ctri"] = ei("d_ctri", [128, 4 * 512])
        for nm in ("wgT", "wuT", "wdT"):
            shp = [HID1, D] if nm == "wdT" else [D, HID1]
            io["f_" + nm] = ei("f_" + nm, shp, I32)
        out = nc.dram_tensor("out", [D, T], F32, kind="ExternalOutput").ap()
        dbg = nc.dram_tensor("dbg", [D, 12 * 128], F32, kind="ExternalOutput").ap()

        s_x0 = nc.dram_tensor("s_x0", [D, W], F32)
        s_att0 = nc.dram_tensor("s_att0", [D, T], F32)
        s_x1 = nc.dram_tensor("s_x1", [D, W], F32)
        s_xm = nc.dram_tensor("s_xm", [D, W], F32)
        s_k0 = nc.dram_tensor("s_k0", [D, W], F32)
        s_k1 = nc.dram_tensor("s_k1", [D, W], F32)
        s_x2 = nc.dram_tensor("s_x2", [D, T], F32)
        s_att1 = nc.dram_tensor("s_att1", [D, T], F32)
        s_x3 = nc.dram_tensor("s_x3", [D, T], F32)

        io["b_xin"] = s_x0.ap()[:, OFF - 1:OFF + T + 1]
        io["b_att"] = s_att0.ap()
        io["b_out"] = s_x1.ap()[:, OFF:OFF + T]
        io["a_att"] = s_att0.ap()
        io["h_xin"] = s_x1.ap()[:, OFF:OFF + T]
        io["h_out"] = s_k0.ap()[:, OFF:OFF + T]
        for li in range(6):
            src = (s_k0 if li % 2 == 0 else s_k1)
            dst = (s_k1 if li % 2 == 0 else s_k0)
            io[f"ca{li}_xin"] = src.ap()[:, OFF - 513:OFF + T]
            io[f"ca{li}_tblq"] = tblq
            io[f"ca{li}_tblk"] = tblk
            io[f"ca{li}_mskw"] = mskw
            io[f"ca{li}_iden"] = iden
            io[f"ca{li}_out"] = s_xm.ap()[:, OFF:OFF + T]
            io[f"cb{li}_xin"] = s_xm.ap()[:, OFF - 1:OFF + T]
            io[f"cb{li}_x1"] = s_x1.ap()[:, OFF:OFF + T] if li == 5 else None
            io[f"cb{li}_out"] = (dst.ap()[:, OFF:OFF + T] if li < 5 else s_x2.ap())
        io["d_x"] = s_x2.ap()
        io["d_att"] = s_att1.ap()
        io["e_att"] = s_att1.ap()
        io["e_x2"] = s_x2.ap()
        io["e_out"] = s_x3.ap()
        io["f_x3"] = s_x3.ap()
        io["f_out"] = out

        fz = {"nc": nc, "p": p, "io": io}

        with ExitStack() as st0:
            zt = st0.enter_context(nc.sbuf_tensor("z_zt", [128, 516], F32))

            def _(v):
                p.i(v.memset(zt[:, :], 0.0), "Ve")
            p.V(_)

            def _(sy):
                p.w(sy, "Ve")
                for scr in (s_x0, s_x1, s_xm, s_k0, s_k1):
                    for ct in range(NCT):
                        r = slice(ct * 128, (ct + 1) * 128)
                        p.i(sy.dma_start(out=scr.ap()[r, 0:OFF], in_=zt[:, 0:OFF]),
                            "Ou", 16)
                        p.i(sy.dma_start(out=scr.ap()[r, OFF + T:W],
                                         in_=zt[:, 0:W - OFF - T],
                                         allow_slow_non_contiguous=True), "Ou", 16)
                p.i(sy.dma_start(out=s_x0.ap()[:, OFF:OFF + T], in_=x_in), "Ou", 16)
            p.S(_)

        tapt = st.enter_context(nc.sbuf_tensor("z_tap", [128, 8 * 16], F32))

        def tapk(idx, ap):
            def _(sy, idx=idx, ap=ap):
                p.w(sy, "Ou")
                p.i(sy.dma_start(out=tapt[:, :].rearrange("p (c n) -> p c n", n=16),
                                 in_=cm(ap)[:, :, 0:16]), "Ld", 16)
                p.w(sy, "Ld")
                p.i(sy.dma_start(out=cm(dbg)[:, :, idx * 128:idx * 128 + 16],
                                 in_=tapt[:, :].rearrange("p (c n) -> p c n", n=16)),
                    "Ou", 16)
            p.S(_)

        ns = [0]

        def barrier():
            sems = ("Ld", "Lg", "Tm", "Ve", "Se", "Pe", "Ou")

            def w_all(e):
                p.w(e, *sems)
            p.S(w_all)
            p.G(w_all)
            p.V(w_all)
            p.A(w_all)
            p.T(w_all)

        def more():
            ns[0] += 1
            if ns[0] <= nstages:
                barrier()
                return True
            return False

        if more():
            build_launch_a(nh=16, fz=fz, pre="a_")
        if more():
            build_launch_b(T, fz=fz, pre="b_")
        if more():
            build_launch_ln(T, fz=fz, pre="h_")
            tapk(0, s_k0.ap()[:, OFF:OFF + 128])
        for li in range(6):
            if more():
                build_launch_ca(T, fz=fz, pre=f"ca{li}_")
                tapk(1 + li, s_xm.ap()[:, OFF:OFF + 128])
            if more():
                build_launch_cb(T, li == 5, fz=fz, pre=f"cb{li}_")
        if more():
            tapk(7, s_x2.ap()[:, 0:128])
            build_launch_d(nh=16, fz=fz, pre="d_")
            tapk(8, s_att1.ap()[:, 0:128])
        if more():
            build_launch_e(T, fz=fz, pre="e_")
            tapk(9, s_x3.ap()[:, 0:128])
        if more():
            build_launch_f(T, fz=fz, pre="f_")

        with ExitStack() as std:
            dt_ = std.enter_context(nc.sbuf_tensor("z_dbg", [128, 128], F32))

            def _(sy):
                p.w(sy, "Ou", "Ld", "Lg", "Ve", "Se", "Tm")
                pass
                p.w(sy, "Ou")
            p.S(_)
    return nc



def host_rotary_tables(t_lo, t_hi):
    nt = t_hi - t_lo
    half = np.arange(0, DH, 2, dtype=np.float32)
    inv_freq = (1.0 / 10000.0 ** (half / DH)).astype(np.float32)
    t = np.arange(t_lo, t_hi, dtype=np.float32)
    freqs = t[:, None] * inv_freq[None, :]
    freqs = np.concatenate([freqs, freqs], -1)
    base_scale = ((half + 0.4 * DH) / (1.4 * DH)).astype(np.float32)
    pw = (t - T // 2) / (WIN // 2)
    scale = base_scale[None, :] ** pw[:, None]
    scale = np.concatenate([scale, scale], -1).astype(np.float32)
    cos = np.cos(freqs).astype(np.float32)
    sin = np.sin(freqs).astype(np.float32)
    sgn = np.ones(DH, np.float32)
    sgn[:32] = -1.0
    inv = (1.0 / scale).astype(np.float32)
    csq = (cos * scale).astype(np.float32)
    snq = (sin * sgn[None, :] * scale).astype(np.float32)
    csk = (cos * inv).astype(np.float32)
    snk = (sin * sgn[None, :] * inv).astype(np.float32)
    tblq = np.concatenate([np.tile(csq, (1, 16)), np.tile(snq, (1, 16))], 1)
    tblk = np.concatenate([np.tile(csk, (1, 16)), np.tile(snk, (1, 16))], 1)
    return np.ascontiguousarray(tblq), np.ascontiguousarray(tblk)


def host_ca_masks(edge):
    pp = np.arange(128)[:, None]
    ff = np.arange(384)[None, :]
    band = np.where((pp <= ff) & (ff <= pp + 256), 0.0, -3e7).astype(np.float32)
    v1 = band.copy()
    v2 = band.copy()
    if edge:
        v1[:, 0:256] = -3e7
        v2[:, 0:128] = -3e7
    return np.ascontiguousarray(np.concatenate([band, v1, v2], axis=1))


def host_d_consts():
    CH = 512
    pp, ff = np.arange(128)[:, None], np.arange(CH)[None, :]
    msk = np.zeros((4, 128, CH), np.float32)
    ctri = np.zeros((4, 128, CH), np.float32)
    for ii, e0 in enumerate((0, 128, 256, 384)):
        msk[ii] = np.where(pp + e0 > ff, np.float32(-3e7), 0.0)
        ctri[ii] = (pp + e0 <= ff).astype(np.float32)
    msk = np.ascontiguousarray(msk.transpose(1, 0, 2).reshape(128, 4 * CH))
    ctri = np.ascontiguousarray(ctri.transpose(1, 0, 2).reshape(128, 4 * CH))
    base_m = np.float32((np.float32(-10000.0 / 10001.0) + np.float32(1.0)) * np.float32(0.5))
    return msk, ctri, np.full((128, 1), base_m, np.float32)


def tok_slice(x_cm, c, halo_l=0, halo_r=0):
    b, t0 = c // 2, (c % 2) * TOK_C
    g0 = b * T + t0
    w = TOK_C + halo_l + halo_r
    o = np.zeros((D, w), np.float32)
    o[:, halo_l:halo_l + TOK_C] = x_cm[:, g0:g0 + TOK_C]
    lo = max(0, halo_l - t0)
    if lo < halo_l:
        o[:, lo:halo_l] = x_cm[:, g0 - (halo_l - lo):g0]
    if t0 + TOK_C < T and halo_r > 0:
        o[:, halo_l + TOK_C:] = x_cm[:, g0 + TOK_C:g0 + TOK_C + halo_r]
    return np.ascontiguousarray(o)


def gather_tok(results, key="out"):
    return np.concatenate([results[c][key] for c in range(CORES)], axis=1)


_NC_CACHE = {}


def _bld(name, fn, *args):
    key = (name,) + args
    if key not in _NC_CACHE:
        _NC_CACHE[key] = fn(*args)
    return _NC_CACHE[key]


def run_chain(inp):
    f32 = lambda a: np.ascontiguousarray(np.asarray(a), dtype=np.float32)
    x_cm = np.ascontiguousarray(
        np.asarray(inp["L0_attn_x"], dtype=np.float32).reshape(NTOK, D).T)

    WqT = f32(np.asarray(inp["L0_attn_Wq"]).T)
    WkT = f32(np.asarray(inp["L0_attn_Wk"]).T)
    WvT = f32(np.asarray(inp["L0_attn_Wv"]).T)
    WpT = f32(np.asarray(inp["L0_attn_Wp"]).T)
    cwb = f32(np.concatenate([np.asarray(inp["L0_attn_conv_w"]),
                              np.asarray(inp["L0_attn_conv_b"])[:, None],
                              np.asarray(inp["L0_attn_bp"])[:, None]], axis=1))
    ncA = _bld("A", build_launch_a)
    im = []
    for c in range(CORES):
        b, hs = c // 2, (c % 2) * 8
        im.append({"x": np.ascontiguousarray(x_cm[:, b * T:(b + 1) * T]),
                   "wqT": np.ascontiguousarray(WqT[:, hs * 64:(hs + 8) * 64]),
                   "wkT": np.ascontiguousarray(WkT[:, hs * 64:(hs + 8) * 64]),
                   "wvT": np.ascontiguousarray(WvT[:, hs * 64:(hs + 8) * 64])})
    res = run_bass_kernel_spmd(ncA, im, core_ids=list(range(CORES))).results
    att_cm = np.zeros((D, NTOK), np.float32)
    for c in range(CORES):
        b, hs = c // 2, (c % 2) * 8
        att_cm[hs * 64:(hs + 8) * 64, b * T:(b + 1) * T] = res[c]["att"]

    ncB = _bld("B", build_launch_b, TOK_C)
    im = []
    for c in range(CORES):
        b, t0 = c // 2, (c % 2) * TOK_C
        g0 = b * T + t0
        im.append({"xin": tok_slice(x_cm, c, 1, 1),
                   "att": np.ascontiguousarray(att_cm[:, g0:g0 + TOK_C]),
                   "wpT": WpT, "cwb": cwb})
    res = run_bass_kernel_spmd(ncB, im, core_ids=list(range(CORES))).results
    x1_cm = gather_tok(res)

    ncH = _bld("H", build_launch_ln, TOK_C)
    im = [{"xin": tok_slice(x1_cm, c)} for c in range(CORES)]
    res = run_bass_kernel_spmd(ncH, im, core_ids=list(range(CORES))).results
    xi_cm = gather_tok(res)

    Wqkv = np.asarray(inp["L0_ffn_Wqkv"])
    Wo0 = np.asarray(inp["L0_ffn_Wo"])
    qsc = np.asarray(inp["L0_ffn_q_scale"])
    ksc = np.asarray(inp["L0_ffn_k_scale"])
    lng = np.asarray(inp["L0_ffn_ln_g"])
    lnb = np.asarray(inp["L0_ffn_ln_b"])
    W1 = np.asarray(inp["L0_ffn_W1"])
    b1 = np.asarray(inp["L0_ffn_b1"])
    W2 = np.asarray(inp["L0_ffn_W2"])
    b2 = np.asarray(inp["L0_ffn_b2"])

    iden = np.eye(128, dtype=np.float32)
    tbls = {}
    for c in range(CORES):
        t0 = (c % 2) * TOK_C
        tbls[c] = host_rotary_tables(t0 - 512, t0 + TOK_C)
    msks = {c: host_ca_masks((c % 2) == 0) for c in range(CORES)}

    ncCa = _bld("Ca", build_launch_ca, TOK_C)
    ncCb = _bld("Cb", build_launch_cb, TOK_C, False)
    ncCbF = _bld("CbF", build_launch_cb, TOK_C, True)

    xcur = xi_cm
    for li in range(6):
        w1p = np.zeros((D, 2 * DIP), np.float32)
        w1p[:, 0:DI] = W1[li][:, 0:DI]
        w1p[:, DIP:DIP + DI] = W1[li][:, DI:2 * DI]
        b1p = np.zeros(2 * DIP, np.float32)
        b1p[0:DI] = b1[li][0:DI]
        b1p[DIP:DIP + DI] = b1[li][DI:2 * DI]
        b1p = np.ascontiguousarray(b1p.reshape(44, 128).T)
        w2p = np.zeros((DIP, D), np.float32)
        w2p[0:DI] = W2[li]
        lgb = np.ascontiguousarray(
            np.stack([lng[li], lnb[li], b2[li]], axis=1).astype(np.float32))
        qsks = np.ascontiguousarray(
            np.concatenate([qsc[li], ksc[li]])[None, :].astype(np.float32))

        im = []
        for c in range(CORES):
            im.append({"xin": tok_slice(xcur, c, 513, 0),
                       "wqkv": f32(Wqkv[li]), "wo": f32(Wo0[li]),
                       "tblq": tbls[c][0], "tblk": tbls[c][1],
                       "qsks": qsks, "mskw": msks[c], "iden": iden})
        res = run_bass_kernel_spmd(ncCa, im, core_ids=list(range(CORES))).results
        xmid = gather_tok(res)

        im = []
        for c in range(CORES):
            d = {"xin": tok_slice(xmid, c, 1, 0),
                 "w1p": f32(w1p), "b1p": f32(b1p), "w2p": f32(w2p), "lgb": lgb}
            if li == 5:
                d["x1"] = tok_slice(x1_cm, c)
            im.append(d)
        ncx = ncCbF if li == 5 else ncCb
        res = run_bass_kernel_spmd(ncx, im, core_ids=list(range(CORES))).results
        xcur = gather_tok(res)
    x2_cm = xcur

    WqT = f32(np.asarray(inp["L1_attn_Wq"]).T)
    WkT = f32(np.asarray(inp["L1_attn_Wk"]).T)
    WvT = f32(np.asarray(inp["L1_attn_Wv"]).T)
    WoT = f32(np.asarray(inp["L1_attn_Wo"]).T)
    slopes = np.asarray(inp["L1_attn_alibi_slopes"], dtype=np.float32)
    power = np.float32(np.asarray(inp["L1_attn_power"]).reshape(-1)[0])
    msk, ctri, bmv = host_d_consts()
    posv = np.arange(T, dtype=np.float32)[None, :]
    pwv = np.full((128, 1), power, np.float32)

    ncD = _bld("D", build_launch_d)
    im = []
    for c in range(CORES):
        b, hs = c // 2, (c % 2) * 8
        im.append({"x": np.ascontiguousarray(x2_cm[:, b * T:(b + 1) * T]),
                   "wqT": np.ascontiguousarray(WqT[:, hs * 64:(hs + 8) * 64]),
                   "wkT": np.ascontiguousarray(WkT[:, hs * 64:(hs + 8) * 64]),
                   "wvT": np.ascontiguousarray(WvT[:, hs * 64:(hs + 8) * 64]),
                   "slp": np.ascontiguousarray(slopes[hs:hs + 8][None, :]),
                   "pwv": pwv, "bmv": bmv, "pos": posv, "msk": msk, "ctri": ctri})
    res = run_bass_kernel_spmd(ncD, im, core_ids=list(range(CORES))).results
    att_cm = np.zeros((D, NTOK), np.float32)
    for c in range(CORES):
        b, hs = c // 2, (c % 2) * 8
        att_cm[hs * 64:(hs + 8) * 64, b * T:(b + 1) * T] = res[c]["att"]

    ncE = _bld("E", build_launch_e, TOK_C)
    im = []
    for c in range(CORES):
        b, t0 = c // 2, (c % 2) * TOK_C
        g0 = b * T + t0
        im.append({"att": np.ascontiguousarray(att_cm[:, g0:g0 + TOK_C]),
                   "x2": tok_slice(x2_cm, c), "woT": WoT})
    res = run_bass_kernel_spmd(ncE, im, core_ids=list(range(CORES))).results
    x3_cm = gather_tok(res)

    wgT = np.ascontiguousarray(np.asarray(inp["L1_ffn_w_gate"]).T)
    wuT = np.ascontiguousarray(np.asarray(inp["L1_ffn_w_up"]).T)
    wdT = np.ascontiguousarray(np.asarray(inp["L1_ffn_w_down"]).T)
    ncF = _bld("F", build_launch_f, TOK_C)
    im = [{"x3": tok_slice(x3_cm, c), "wgT": wgT, "wuT": wuT, "wdT": wdT}
          for c in range(CORES)]
    res = run_bass_kernel_spmd(ncF, im, core_ids=list(range(CORES))).results
    return gather_tok(res)


def run_fused(inp):
    f32 = lambda a: np.ascontiguousarray(np.asarray(a), dtype=np.float32)
    x_cm = np.ascontiguousarray(
        np.asarray(inp["L0_attn_x"], dtype=np.float32).reshape(NTOK, D).T)
    nc = _bld("FZ", build_fused)
    tblq, tblk = host_rotary_tables(-512, T)
    mskw = host_ca_masks(True)
    iden = np.eye(128, dtype=np.float32)
    msk, ctri, bmv = host_d_consts()
    power = np.float32(np.asarray(inp["L1_attn_power"]).reshape(-1)[0])
    base = {
        "a_wqT": f32(np.asarray(inp["L0_attn_Wq"]).T),
        "a_wkT": f32(np.asarray(inp["L0_attn_Wk"]).T),
        "a_wvT": f32(np.asarray(inp["L0_attn_Wv"]).T),
        "b_wpT": f32(np.asarray(inp["L0_attn_Wp"]).T),
        "b_cwb": f32(np.concatenate([np.asarray(inp["L0_attn_conv_w"]),
                                     np.asarray(inp["L0_attn_conv_b"])[:, None],
                                     np.asarray(inp["L0_attn_bp"])[:, None]], axis=1)),
        "tblq": tblq, "tblk": tblk, "mskw": mskw, "iden": iden,
        "l1_wqT": f32(np.asarray(inp["L1_attn_Wq"]).T),
        "l1_wkT": f32(np.asarray(inp["L1_attn_Wk"]).T),
        "l1_wvT": f32(np.asarray(inp["L1_attn_Wv"]).T),
        "l1_woT": f32(np.asarray(inp["L1_attn_Wo"]).T),
        "d_slp": f32(np.asarray(inp["L1_attn_alibi_slopes"])[None, :]),
        "d_pwv": np.full((128, 1), power, np.float32),
        "d_bmv": bmv,
        "d_pos": np.arange(T, dtype=np.float32)[None, :],
        "d_msk": msk, "d_ctri": ctri,
        "f_wgT": np.ascontiguousarray(np.asarray(inp["L1_ffn_w_gate"]).T),
        "f_wuT": np.ascontiguousarray(np.asarray(inp["L1_ffn_w_up"]).T),
        "f_wdT": np.ascontiguousarray(np.asarray(inp["L1_ffn_w_down"]).T),
    }
    W1 = np.asarray(inp["L0_ffn_W1"])
    b1 = np.asarray(inp["L0_ffn_b1"])
    W2 = np.asarray(inp["L0_ffn_W2"])
    for li in range(6):
        base[f"ca{li}_wqkv"] = f32(np.asarray(inp["L0_ffn_Wqkv"])[li])
        base[f"ca{li}_wo"] = f32(np.asarray(inp["L0_ffn_Wo"])[li])
        base[f"ca{li}_qsks"] = f32(np.concatenate(
            [np.asarray(inp["L0_ffn_q_scale"])[li],
             np.asarray(inp["L0_ffn_k_scale"])[li]])[None, :])
        w1p = np.zeros((D, 2 * DIP), np.float32)
        w1p[:, 0:DI] = W1[li][:, 0:DI]
        w1p[:, DIP:DIP + DI] = W1[li][:, DI:2 * DI]
        b1p = np.zeros(2 * DIP, np.float32)
        b1p[0:DI] = b1[li][0:DI]
        b1p[DIP:DIP + DI] = b1[li][DI:2 * DI]
        base[f"cb{li}_w1p"] = w1p
        base[f"cb{li}_b1p"] = np.ascontiguousarray(b1p.reshape(44, 128).T)
        w2p = np.zeros((DIP, D), np.float32)
        w2p[0:DI] = W2[li]
        base[f"cb{li}_w2p"] = w2p
        base[f"cb{li}_lgb"] = f32(np.stack(
            [np.asarray(inp["L0_ffn_ln_g"])[li], np.asarray(inp["L0_ffn_ln_b"])[li],
             np.asarray(inp["L0_ffn_b2"])[li]], axis=1))
    im = []
    for c in range(CORES):
        b = c // 2
        d = dict(base)
        d["x"] = np.ascontiguousarray(x_cm[:, b * T:(b + 1) * T])
        im.append(d)
    res = run_bass_kernel_spmd(nc, im, core_ids=list(range(CORES))).results
    global FZ_DBG
    FZ_DBG = [res[c].get("dbg") for c in range(CORES)]
    x4_cm = np.concatenate([res[2 * b]["out"] for b in range(B)], axis=1)
    return x4_cm


FZ_DBG = None


USE_FUSED = True


def kernel(**inp):
    x4_cm = run_fused(inp) if USE_FUSED else run_chain(inp)
    return np.ascontiguousarray(x4_cm.T).reshape(B, T, D)
